# version 26
# baseline (speedup 1.0000x reference)
import math
from contextlib import ExitStack

import numpy as np
import ml_dtypes
import concourse.bass as bass
import concourse.mybir as mybir
from concourse.bass_utils import run_bass_kernel_spmd

F32 = mybir.dt.float32
BF16 = mybir.dt.bfloat16
I32 = mybir.dt.int32
AF = mybir.ActivationFunctionType
ALU = mybir.AluOpType
AX = mybir.AxisListType

SEQS = 4
N = 2048
D = 1024
NT = 16
NG = 4
HEADS = 16
DH = 64
HID = 3072
DCH = 8
HCH = 24
ROWS = SEQS * N

EPS_R = float(np.finfo(np.float32).eps)
C1_RMS = 1.0 + EPS_R
C2_RMS = 1e-5 * EPS_R
C1_LN = 1.0
C2_LN = 1e-5
QK_SCALE = 0.125
DEQ = 1.0 / 127.0
DEQ2 = 1.0 / (127.0 * 127.0)
NEGBIG = -1.0e30

ENGINES = ("sync", "gpsimd", "vector", "scalar", "tensor")
DEBUG_TAPS = False
ATTN_DEFER = True
NREG = 5
USE_BN = True
TBUF2 = True


class Emitter:
    def __init__(self, sems):
        self.sems = sems
        self.events = {k: [] for k in ENGINES}
        self.cnt = {k: 0 for k in sems}

    def op(self, engk, builder, waits=(), inc=None, k=1):
        tgt = None
        if inc is not None:
            self.cnt[inc] += k
            tgt = self.cnt[inc]
        w = tuple((s, v) for (s, v) in waits if v and v > 0)

        def fn(eng, _w=w, _b=builder, _i=inc, _k=k):
            for s, v in _w:
                eng.wait_ge(self.sems[s], v)
            r = _b(eng) if _b is not None else None
            if _i is not None:
                if r is None:
                    r = eng.nop()
                r.then_inc(self.sems[_i], _k)

        self.events[engk].append(fn)
        return tgt

    def replay(self, block):
        for key, reg in (("sync", block.sync), ("gpsimd", block.gpsimd),
                         ("vector", block.vector), ("scalar", block.scalar),
                         ("tensor", block.tensor)):
            evs = self.events[key]

            def body(eng, _evs=evs):
                for f in _evs:
                    f(eng)

            reg(body)
        self.events = {k: [] for k in ENGINES}


def ceil_div(a, b):
    return (a + b - 1) // b


def build_program():
    nc = bass.Bass("TRN2", target_bir_lowering=False, debug=False,
                   detect_race_conditions=False)

    x_h = nc.dram_tensor("x", [ROWS, D], F32, kind="ExternalInput").ap()
    nw_h = nc.dram_tensor("norm_w", [D, 1], F32, kind="ExternalInput").ap()
    wq_h = nc.dram_tensor("Wq", [D, D], F32, kind="ExternalInput").ap()
    wkv_h = nc.dram_tensor("Wkv", [D, 2 * D], F32, kind="ExternalInput").ap()
    wo_h = nc.dram_tensor("Wout", [D, D], F32, kind="ExternalInput").ap()
    wg_h = nc.dram_tensor("Wg", [D, HEADS], F32, kind="ExternalInput").ap()
    wgi_h = nc.dram_tensor("w_gate", [HID, D], I32, kind="ExternalInput").ap()
    wui_h = nc.dram_tensor("w_up", [HID, D], I32, kind="ExternalInput").ap()
    wdi_h = nc.dram_tensor("w_down", [D, HID], I32, kind="ExternalInput").ap()
    y_h = nc.dram_tensor("y", [ROWS, D], F32, kind="ExternalOutput").ap()

    _k = "ExternalOutput" if DEBUG_TAPS else "Internal"
    sc_xp = nc.dram_tensor("sc_xp", [ROWS, D], BF16, kind=_k).ap()
    sc_ao = nc.dram_tensor("sc_ao", [D, ROWS], BF16, kind=_k).ap()
    sc_xm = nc.dram_tensor("sc_xm", [ROWS, D], F32, kind=_k).ap()
    sc_wg = nc.dram_tensor("sc_wg", [HID, D], BF16, kind="Internal").ap()
    sc_wu = nc.dram_tensor("sc_wu", [HID, D], BF16, kind="Internal").ap()
    sc_wd = nc.dram_tensor("sc_wd", [D, HID], BF16, kind="Internal").ap()

    ident_c = nc.inline_tensor(np.eye(128, dtype=ml_dtypes.bfloat16), name="identc")
    maskt_c = nc.inline_tensor(
        np.tril(np.full((128, 128), NEGBIG, np.float32), -1).astype(ml_dtypes.bfloat16),
        name="masktc")

    with ExitStack() as es:
        sems = {}
        for snm in ("HW", "GP", "PE", "DV", "AC", "PH", "PL"):
            sems[snm] = es.enter_context(nc.semaphore("s" + snm))
        E = Emitter(sems)
        lm = {}

        def sb(stack, name, shape, dtype, **kw):
            return stack.enter_context(nc.sbuf_tensor(name, shape, dtype, **kw))

        def ps(stack, name, shape):
            return stack.enter_context(nc.psum_tensor(name, shape, F32))

        xt_sb = sb(es, "xt", [128, 2 * D], F32)
        xp_sb = sb(es, "xp", [128, 2 * D], BF16)
        dump_sb = sb(es, "dump", [128, D], BF16)
        dump2_sb = (sb(es, "dump2", [128, D], BF16)
                    if not USE_BN else dump_sb)
        ln_sb = sb(es, "ln", [128, 2 * D], F32)
        aot_sb = sb(es, "aot", [128, 2 * D], BF16, is_transpose_dest=True)
        aoT_sb = sb(es, "aoT", [128, 2 * 512], BF16)
        st_sb = sb(es, "stats", [128, 128], F32)

        def stc(idx):
            return st_sb[:, idx:idx + 1]

        def st2(a, b):
            return st_sb[:, a:b]


        def pget(d, s, t):
            if t >= 2:
                return d.get((s, t - 2), 0)
            if s > 0:
                return d.get((s - 1, NT + t - 2), 0)
            return 0

        with ExitStack() as ea:
            wq_sb = sb(ea, "wq", [128, DCH * D], BF16)
            wkv_sb = sb(ea, "wkv", [128, DCH * 2 * D], BF16)
            wo_sb = sb(ea, "wo", [128, DCH * D], BF16)
            wg_sb = sb(ea, "wgt", [128, DCH * HEADS], BF16)
            nw_sb = sb(ea, "nw", [128, DCH], F32)
            id_sb = sb(ea, "id", [128, 128], BF16)
            mk_sb = sb(ea, "mk", [128, 128], BF16)
            stage_sb = sb(ea, "stage", [128, 3072], BF16)
            xfm_sb = sb(ea, "xfm", [128, DCH * 512], BF16, is_transpose_dest=True)
            qT_sb = sb(ea, "qT", [128, DCH * 512], BF16)
            kT_sb = sb(ea, "kT", [128, DCH * N], BF16)
            v_sb = sb(ea, "v", [128, NT * D], BF16)
            ga_sb = sb(ea, "ga", [128, NT * HEADS], F32)
            at_sb = sb(ea, "at", [128, 2 * N], BF16)
            aT_sb = sb(ea, "aT", [128, N], BF16)
            d_sb = sb(ea, "d", [128, 2 * 128], BF16)
            of_sb = sb(ea, "of", [128, DCH * 512], BF16)

            psR = [ps(ea, "psR0", [128, 512]), ps(ea, "psR1", [128, 512]),
                   ps(ea, "psR2", [128, 512]), ps(ea, "psR3", [128, 512])]
            psT = ps(ea, "psT", [128, 512])
            psAV = ps(ea, "psAV", [128, 128])
            psA = [ps(ea, "psA0", [128, 512]), ps(ea, "psA1", [128, 512])]
            sreg = psR + [psA[1]]

            E.op("sync", lambda e: e.dma_start(out=id_sb[:], in_=ident_c.ap()[:, :]),
                 inc="HW", k=16)
            E.op("sync", lambda e: e.dma_start(out=mk_sb[:], in_=maskt_c.ap()[:, :]),
                 inc="HW", k=16)
            for dc in range(DCH):
                E.op("sync",
                     lambda e, dc=dc: e.dma_start(
                         out=nw_sb[:, dc:dc + 1], in_=nw_h[dc * 128:(dc + 1) * 128, 0:1]),
                     inc="HW", k=16)
            lm["consts"] = E.cnt["HW"]

            E.op("gpsimd", lambda e: e.dma_start(
                out=wq_sb.ap().rearrange("p (c j) -> p c j", c=DCH),
                in_=wq_h.rearrange("(c p) j -> p c j", p=128)), inc="GP", k=16)
            E.op("gpsimd", lambda e: e.dma_start(
                out=wkv_sb.ap().rearrange("p (c j) -> p c j", c=DCH),
                in_=wkv_h.rearrange("(c p) j -> p c j", p=128)), inc="GP", k=16)
            E.op("gpsimd", lambda e: e.dma_start(
                out=wo_sb.ap().rearrange("p (c j) -> p c j", c=DCH),
                in_=wo_h.rearrange("(c p) j -> p c j", p=128)), inc="GP", k=16)
            E.op("gpsimd", lambda e: e.dma_start(
                out=wg_sb.ap().rearrange("p (c j) -> p c j", c=DCH),
                in_=wg_h.rearrange("(c p) j -> p c j", p=128)), inc="GP", k=16)
            lm["wcast"] = E.cnt["GP"]

            for dc in range(DCH):
                E.op("vector",
                     lambda e, dc=dc: e.tensor_scalar(
                         out=wq_sb[:, dc * D:(dc + 1) * D],
                         in0=wq_sb[:, dc * D:(dc + 1) * D],
                         scalar1=nw_sb[:, dc:dc + 1], scalar2=None, op0=ALU.mult),
                     waits=[("GP", lm["wcast"]), ("HW", lm["consts"])])
            for dc in range(DCH):
                E.op("vector",
                     lambda e, dc=dc: e.tensor_scalar(
                         out=wkv_sb[:, dc * 2 * D:(dc + 1) * 2 * D],
                         in0=wkv_sb[:, dc * 2 * D:(dc + 1) * 2 * D],
                         scalar1=nw_sb[:, dc:dc + 1], scalar2=None, op0=ALU.mult))
            for dc in range(DCH):
                E.op("vector",
                     lambda e, dc=dc: e.tensor_scalar(
                         out=wg_sb[:, dc * HEADS:(dc + 1) * HEADS],
                         in0=wg_sb[:, dc * HEADS:(dc + 1) * HEADS],
                         scalar1=nw_sb[:, dc:dc + 1], scalar2=None, op0=ALU.mult),
                     inc=("DV" if dc == DCH - 1 else None))
            lm["wscaled"] = E.cnt["DV"]

            prep_state = {"prev_store": 0}
            prep_rounds = []
            for srcw, dstw, nrounds, rows in ((wgi_h, sc_wg, 8, 384),
                                              (wui_h, sc_wu, 8, 384),
                                              (wdi_h, sc_wd, 8, 128)):
                for r in range(nrounds):
                    prep_rounds.append((srcw, dstw, r, rows))

            def emit_prep(k):
                while k > 0 and prep_rounds:
                    srcw, dstw, r, rows = prep_rounds.pop(0)
                    if rows == 384:
                        cast = E.op("gpsimd",
                                    lambda e, srcw=srcw, r=r: e.dma_start(
                                        out=stage_sb.ap().rearrange(
                                            "p (c d) -> p c d", c=3),
                                        in_=srcw[r * 384:(r + 1) * 384, :].rearrange(
                                            "(c p) d -> p c d", p=128)),
                                    waits=[("GP", prep_state["prev_store"])],
                                    inc="GP", k=16)
                        prep_state["prev_store"] = E.op(
                            "gpsimd",
                            lambda e, dstw=dstw, r=r: e.dma_start(
                                out=dstw[r * 384:(r + 1) * 384, :].rearrange(
                                    "(c p) d -> p c d", p=128),
                                in_=stage_sb.ap().rearrange(
                                    "p (c d) -> p c d", c=3)),
                            waits=[("GP", cast)], inc="GP", k=16)
                    else:
                        cast = E.op("gpsimd",
                                    lambda e, srcw=srcw, r=r: e.dma_start(
                                        out=stage_sb[:, 0:HID],
                                        in_=srcw[r * 128:(r + 1) * 128, :]),
                                    waits=[("GP", prep_state["prev_store"])],
                                    inc="GP", k=16)
                        prep_state["prev_store"] = E.op(
                            "gpsimd",
                            lambda e, dstw=dstw, r=r: e.dma_start(
                                out=dstw[r * 128:(r + 1) * 128, :],
                                in_=stage_sb[:, 0:HID]),
                            waits=[("GP", cast)], inc="GP", k=16)
                    k -= 1

            def ln_stats(src_ap_fn, par, c1, c2, waits_first, base=0):
                if USE_BN:
                    return ln_stats_bn(src_ap_fn, par, c1, c2, waits_first, base)
                return ln_stats_sq(src_ap_fn, par, c1, c2, waits_first, base)

            def ln_stats_bn(src_ap_fn, par, c1, c2, waits_first, base=0):
                b = base + par * 16
                v1 = E.op("vector",
                          lambda e: (e.bn_stats(out=st2(b, b + 6),
                                                in_=src_ap_fn()[:, 0:512]),
                                     e.bn_stats(out=st2(b + 6, b + 12),
                                                in_=src_ap_fn()[:, 512:1024]))[-1],
                          waits=waits_first, inc="DV")
                v2 = E.op("vector",
                          lambda e: e.bn_aggr(out=st2(b + 12, b + 14),
                                              in_=st2(b, b + 12)),
                          waits=[("DV", v1)], inc="DV")
                v3 = E.op("vector",
                          lambda e: e.tensor_scalar(out=stc(b + 13),
                                                    in0=stc(b + 13),
                                                    scalar1=c1, scalar2=c2,
                                                    op0=ALU.mult, op1=ALU.add),
                          waits=[("DV", v2)], inc="DV")
                a2 = E.op("scalar",
                          lambda e: e.activation(out=stc(b + 14), in_=stc(b + 13),
                                                 func=AF.Ln),
                          waits=[("DV", v3)], inc="AC")
                a3 = E.op("scalar",
                          lambda e: e.activation(out=stc(b + 14), in_=stc(b + 14),
                                                 func=AF.Exp, scale=-0.5),
                          waits=[("AC", a2)], inc="AC")
                stat = E.op("vector",
                            lambda e: e.tensor_scalar(out=stc(b + 15),
                                                      in0=stc(b + 12),
                                                      scalar1=stc(b + 14),
                                                      scalar2=-1.0,
                                                      op0=ALU.mult,
                                                      op1=ALU.mult),
                            waits=[("AC", a3)], inc="DV")
                return stat, stc(b + 14), stc(b + 15)

            def ln_stats_sq(src_ap_fn, par, c1, c2, waits_first, base=0):
                b = base + par * 8
                sq = E.op("scalar",
                          lambda e: e.activation(out=dump2_sb[:, :],
                                                 in_=src_ap_fn(),
                                                 func=AF.Square,
                                                 accum_out=stc(b + 1)),
                          waits=waits_first, inc="AC")
                v1 = E.op("vector",
                          lambda e: e.reduce_sum(out=stc(b + 0), in_=src_ap_fn(),
                                                 axis=AX.X, negate=True),
                          waits=waits_first, inc="DV")
                v2 = E.op("vector",
                          lambda e: e.tensor_scalar(out=stc(b + 2), in0=stc(b + 1),
                                                    scalar1=1.0 / D, scalar2=None,
                                                    op0=ALU.mult),
                          waits=[("AC", sq)], inc="DV")
                v3 = E.op("vector",
                          lambda e: e.tensor_scalar(out=stc(b + 3), in0=stc(b + 0),
                                                    scalar1=stc(b + 0),
                                                    scalar2=1.0 / (D * D),
                                                    op0=ALU.mult, op1=ALU.mult),
                          waits=[("DV", v1)], inc="DV")
                v4 = E.op("vector",
                          lambda e: e.tensor_tensor(out=stc(b + 2), in0=stc(b + 2),
                                                    in1=stc(b + 3), op=ALU.subtract),
                          waits=[("DV", v3)], inc="DV")
                v5 = E.op("vector",
                          lambda e: e.tensor_scalar(out=stc(b + 2), in0=stc(b + 2),
                                                    scalar1=c1, scalar2=c2,
                                                    op0=ALU.mult, op1=ALU.add),
                          waits=[("DV", v4)], inc="DV")
                a2 = E.op("scalar",
                          lambda e: e.activation(out=stc(b + 4), in_=stc(b + 2),
                                                 func=AF.Ln),
                          waits=[("DV", v5)], inc="AC")
                a3 = E.op("scalar",
                          lambda e: e.activation(out=stc(b + 4), in_=stc(b + 4),
                                                 func=AF.Exp, scale=-0.5),
                          waits=[("AC", a2)], inc="AC")
                stat = E.op("vector",
                            lambda e: e.tensor_scalar(out=stc(b + 5), in0=stc(b + 0),
                                                      scalar1=stc(b + 4),
                                                      scalar2=1.0 / D,
                                                      op0=ALU.mult,
                                                      op1=ALU.mult),
                            waits=[("AC", a3)], inc="DV")
                return stat, stc(b + 4), stc(b + 5)

            prev_qkv_done = 0
            prev_avd_last = 0
            prev_dd_last = 0
            exp_region = [0, 0, 0, 0, 0]
            tb_last_par = {0: 0, 1: 0}
            u4 = 0
            rowc = 0
            evt_alt = 0
            prev_evav = 0
            tbuf_free = {0: ("DV", 0), 1: ("DV", 0)}
            prev_pj_done = 0
            prev_aost = 0
            last_aost = {0: 0, 1: 0}
            prev_psA_free = {}
            lm_norm = {}
            lm_xpst = {}
            lm_lnA = {}
            lm_xmst = {}

            def emit_a1(s, t):
                if True:
                    row0 = s * N + t * 128
                    par = t % 2
                    xsl = (slice(None), slice(par * D, (par + 1) * D))
                    ld = E.op("sync",
                              lambda e, row0=row0, xsl=xsl: e.dma_start(
                                  out=xt_sb[xsl], in_=x_h[row0:row0 + 128, :]),
                              waits=[("AC", pget(lm_norm, s, t))],
                              inc="HW", k=16)
                    stat, sg, bi = ln_stats(
                        lambda xsl=xsl: xt_sb[xsl], par, C1_RMS, C2_RMS,
                        [("HW", ld)], base=32)
                    nrm = E.op("scalar",
                               lambda e, xsl=xsl, par=par, sg=sg, bi=bi:
                               e.activation(out=xp_sb[:, par * D:(par + 1) * D],
                                            in_=xt_sb[xsl], func=AF.Identity,
                                            bias=bi, scale=sg),
                               waits=[("DV", stat),
                                      ("GP", pget(lm_xpst, s, t))],
                               inc="AC")
                    lm_norm[(s, t)] = nrm
                    lm_xpst[(s, t)] = E.op(
                        "gpsimd",
                        lambda e, row0=row0, par=par: e.dma_start(
                            out=sc_xp[row0:row0 + 128, :],
                            in_=xp_sb[:, par * D:(par + 1) * D]),
                        waits=[("AC", nrm)], inc="GP", k=16)

            for t in range(NT):
                emit_a1(0, t)
            for s in range(SEQS):
                for g in range(NG):
                    grow = s * N + g * 512
                    for dc in range(DCH):
                        h = E.op("sync",
                                 lambda e, dc=dc, grow=grow: e.dma_start_transpose(
                                     out=xfm_sb[:, dc * 512:(dc + 1) * 512],
                                     in_=sc_xp[grow:grow + 512,
                                               dc * 128:(dc + 1) * 128]),
                                 waits=[("GP", lm_xpst[(s, 4 * g + 3)]),
                                        ("PE", prev_qkv_done)],
                                 inc="HW", k=16)
                    lm_xfm = h

                    units = ([("q", hc) for hc in range(DCH)]
                             + [("k", hc) for hc in range(DCH)]
                             + [("v", tt, nn) for tt in range(4) for nn in range(2)]
                             + [("g", tt) for tt in range(4)])
                    for ui, u in enumerate(units):
                        bank = psA[ui % 2]
                        free_w = [prev_psA_free.get(ui % 2, ("DV", 0))]
                        for dc in range(DCH):
                            if u[0] == "q":
                                lhsT = wq_sb[:, dc * D + u[1] * 128:
                                             dc * D + (u[1] + 1) * 128]
                                rhs = xfm_sb[:, dc * 512:(dc + 1) * 512]
                                oap = bank[:, 0:512]
                            elif u[0] == "k":
                                lhsT = wkv_sb[:, dc * 2 * D + u[1] * 128:
                                              dc * 2 * D + (u[1] + 1) * 128]
                                rhs = xfm_sb[:, dc * 512:(dc + 1) * 512]
                                oap = bank[:, 0:512]
                            elif u[0] == "v":
                                tt, nn = u[1], u[2]
                                lhsT = xfm_sb[:, dc * 512 + tt * 128:
                                              dc * 512 + (tt + 1) * 128]
                                rhs = wkv_sb[:, dc * 2 * D + D + nn * 512:
                                             dc * 2 * D + D + (nn + 1) * 512]
                                oap = bank[:, 0:512]
                            else:
                                tt = u[1]
                                lhsT = xfm_sb[:, dc * 512 + tt * 128:
                                              dc * 512 + (tt + 1) * 128]
                                rhs = wg_sb[:, dc * HEADS:(dc + 1) * HEADS]
                                oap = bank[:, 0:HEADS]
                            w = ([("HW", lm_xfm), ("DV", lm["wscaled"])]
                                 + [(sm, v) for (sm, v) in free_w]
                                 ) if dc == 0 else []
                            mmlm = E.op("tensor",
                                        lambda e, o=oap, l=lhsT, r=rhs, dc=dc:
                                        e.matmul(o, l, r, start=(dc == 0),
                                                 stop=(dc == DCH - 1)),
                                        waits=w,
                                        inc=("PE" if dc == DCH - 1 else None))
                        ev_eng = "scalar" if (u[0] == "g" or ui % 2 == 1) else "vector"
                        if u[0] == "q":
                            dst = qT_sb[:, u[1] * 512:(u[1] + 1) * 512]
                        elif u[0] == "k":
                            dst = kT_sb[:, u[1] * N + g * 512:
                                        u[1] * N + (g + 1) * 512]
                        elif u[0] == "v":
                            tt, nn = u[1], u[2]
                            dst = v_sb[:, (4 * g + tt) * D + nn * 512:
                                       (4 * g + tt) * D + (nn + 1) * 512]
                        else:
                            tt = u[1]
                            dst = ga_sb[:, (4 * g + tt) * HEADS:
                                        (4 * g + tt + 1) * HEADS]
                        wz = [("PE", mmlm), ("PE", prev_avd_last)]
                        if u[0] == "g":
                            wz.append(("DV", prev_dd_last))
                            evlm = E.op("scalar",
                                        lambda e, dst=dst, bank=bank: e.activation(
                                            out=dst, in_=bank[:, 0:HEADS],
                                            func=AF.Exp, scale=-1.0),
                                        waits=wz, inc="AC")
                            g1 = E.op("vector",
                                      lambda e, dst=dst: e.tensor_scalar(
                                          out=dst, in0=dst, scalar1=1.0,
                                          scalar2=None, op0=ALU.add),
                                      waits=[("AC", evlm)], inc="DV")
                            E.op("vector",
                                 lambda e, dst=dst: e.reciprocal(
                                     out=dst, in_=dst),
                                 waits=[("DV", g1)], inc="DV")
                            prev_psA_free[ui % 2] = ("AC", evlm)
                        else:
                            if ev_eng == "vector":
                                evlm = E.op("vector",
                                            lambda e, dst=dst, bank=bank:
                                            e.tensor_copy(out=dst,
                                                          in_=bank[:, 0:512]),
                                            waits=wz, inc="DV")
                                prev_psA_free[ui % 2] = ("DV", evlm)
                            else:
                                evlm = E.op("scalar",
                                            lambda e, dst=dst, bank=bank:
                                            e.activation(out=dst,
                                                         in_=bank[:, 0:512],
                                                         func=AF.Identity),
                                            waits=wz, inc="AC")
                                prev_psA_free[ui % 2] = ("AC", evlm)
                        if ui == len(units) - 1:
                            prev_qkv_done = mmlm

                    pend_b = None
                    pend_cm = None
                    pend_c = None
                    psA0_extra = [prev_psA_free.get(0, ("DV", 0))]
                    psA1_extra = [prev_psA_free.get(1, ("DV", 0))]
                    for tt in range(4):
                        t = 4 * g + tt
                        for hh in range(HEADS):
                            new_c = None
                            if pend_b is not None:
                                dd_prev = pend_b()
                                new_c = (pend_cm, dd_prev)
                            old_c = pend_c
                            pend_c = new_c
                            L = (t + 1) * 128
                            rp = rowc % 2
                            hb = (hh % 2) * 64
                            hc = hh // 2
                            qsl = (slice(hb, hb + 64),
                                   slice(hc * 512 + tt * 128,
                                         hc * 512 + (tt + 1) * 128))
                            nu = ceil_div(L, 512)
                            uinf = []
                            for ui in range(nu):
                                c_lo = ui * 512
                                c_hi = min(L, c_lo + 512)
                                ri = u4 % NREG
                                Lu = c_hi - c_lo
                                nmc = 0 + rp * 4 + ui
                                rsc = 16 + rp * 4 + ui
                                has_diag = (c_hi == L)
                                fw = [("AC", exp_region[ri])]
                                if ri == 4 and psA1_extra:
                                    fw.append(psA1_extra.pop())
                                if has_diag:
                                    if Lu > 128:
                                        E.op("tensor",
                                             lambda e, ri=ri, Lu=Lu, c_lo=c_lo,
                                             qsl=qsl, hb=hb, hc=hc: e.matmul(
                                                 sreg[ri][:, 0:Lu - 128],
                                                 qT_sb[qsl],
                                                 kT_sb[hb:hb + 64,
                                                       hc * N + c_lo:
                                                       hc * N + c_lo + Lu - 128],
                                                 start=True, stop=True),
                                             waits=fw)
                                        fw = []
                                    E.op("tensor",
                                         lambda e, ri=ri, Lu=Lu: e.matmul(
                                             sreg[ri][:, Lu - 128:Lu],
                                             mk_sb[:], id_sb[:],
                                             start=True, stop=False),
                                         waits=fw)
                                    smm = E.op(
                                        "tensor",
                                        lambda e, ri=ri, Lu=Lu, c_lo=c_lo,
                                        qsl=qsl, hb=hb, hc=hc: e.matmul(
                                            sreg[ri][:, Lu - 128:Lu],
                                            qT_sb[qsl],
                                            kT_sb[hb:hb + 64,
                                                  hc * N + c_lo + Lu - 128:
                                                  hc * N + c_lo + Lu],
                                            start=False, stop=True),
                                        inc="PE")
                                else:
                                    smm = E.op(
                                        "tensor",
                                        lambda e, ri=ri, Lu=Lu, c_lo=c_lo,
                                        qsl=qsl, hb=hb, hc=hc: e.matmul(
                                            sreg[ri][:, 0:Lu], qT_sb[qsl],
                                            kT_sb[hb:hb + 64,
                                                  hc * N + c_lo:hc * N + c_lo + Lu],
                                            start=True, stop=True),
                                        waits=fw, inc="PE")
                                mxl = E.op(
                                    "vector",
                                    lambda e, ri=ri, Lu=Lu, nmc=nmc:
                                    e.reduce_max(out=stc(nmc),
                                                 in_=sreg[ri][:, 0:Lu],
                                                 axis=AX.X, negate=True),
                                    waits=[("PE", smm)], inc="DV")
                                uinf.append((ri, nmc, rsc, c_lo, c_hi, mxl))
                                u4 += 1
                            bc = 8 + rp
                            sc_ = 10 + rp
                            cc_ = 12 + rp
                            if nu == 1:
                                bias_lm = E.op(
                                    "vector",
                                    lambda e, u=uinf[0], bc=bc: e.tensor_scalar(
                                        out=stc(bc), in0=stc(u[1]),
                                        scalar1=QK_SCALE, scalar2=None,
                                        op0=ALU.mult),
                                    waits=[("DV", uinf[0][5])], inc="DV")
                            else:
                                t0 = E.op(
                                    "vector",
                                    lambda e, rp=rp, nu=nu, bc=bc:
                                    e.tensor_reduce(
                                        out=stc(bc),
                                        in_=st_sb[:, 0 + rp * 4:
                                                  0 + rp * 4 + nu],
                                        axis=AX.X, op=ALU.min),
                                    waits=[("DV", uinf[-1][5])], inc="DV")
                                bias_lm = E.op(
                                    "vector",
                                    lambda e, bc=bc: e.tensor_scalar(
                                        out=stc(bc), in0=stc(bc),
                                        scalar1=QK_SCALE, scalar2=None,
                                        op0=ALU.mult),
                                    waits=[("DV", t0)], inc="DV")

                            def mk_stage_b(rp=rp, nu=nu, uinf=uinf, bc=bc,
                                           sc_=sc_, cc_=cc_, bias_lm=bias_lm,
                                           t=t, hh=hh):
                                def stage_b():
                                    last_exp = 0
                                    for (ri, nmc, rsc, c_lo, c_hi, _) in uinf:
                                        last_exp = E.op(
                                            "scalar",
                                            lambda e, ri=ri, rsc=rsc, c_lo=c_lo,
                                            c_hi=c_hi, rp=rp, bc=bc:
                                            e.activation(
                                                out=at_sb[:, rp * N + c_lo:
                                                          rp * N + c_hi],
                                                in_=sreg[ri][:, 0:c_hi - c_lo],
                                                func=AF.Exp, scale=QK_SCALE,
                                                bias=stc(bc),
                                                accum_out=stc(rsc)),
                                            waits=[("DV", bias_lm),
                                                   ("PE", tb_last_par[rp])],
                                            inc="AC")
                                        exp_region[ri] = last_exp
                                    if nu > 1:
                                        s0 = E.op(
                                            "vector",
                                            lambda e, rp=rp, nu=nu, sc_=sc_:
                                            e.reduce_sum(
                                                out=stc(sc_),
                                                in_=st_sb[:, 16 + rp * 4:
                                                          16 + rp * 4 + nu],
                                                axis=AX.X),
                                            waits=[("AC", last_exp)], inc="DV")
                                        r0 = E.op(
                                            "vector",
                                            lambda e, sc_=sc_, cc_=cc_:
                                            e.reciprocal(
                                                out=stc(cc_), in_=stc(sc_)),
                                            waits=[("DV", s0)], inc="DV")
                                    else:
                                        r0 = E.op(
                                            "vector",
                                            lambda e, u0=uinf[0], cc_=cc_:
                                            e.reciprocal(
                                                out=stc(cc_), in_=stc(u0[2])),
                                            waits=[("AC", last_exp)], inc="DV")
                                    dd = E.op(
                                        "vector",
                                        lambda e, cc_=cc_, rp=rp, t=t, hh=hh:
                                        e.tensor_scalar(
                                            out=d_sb[:, rp * 128:(rp + 1) * 128],
                                            in0=id_sb[:], scalar1=stc(cc_),
                                            scalar2=ga_sb[:, t * HEADS + hh:
                                                          t * HEADS + hh + 1],
                                            op0=ALU.mult, op1=ALU.mult),
                                        waits=[("DV", r0),
                                               ("PE", tb_last_par[rp])],
                                        inc="DV")
                                    return dd
                                return stage_b

                            def mk_stage_c(t=t, hh=hh, hb=hb, hc=hc, tt=tt,
                                           rp=rp):
                                def stage_c(dd):
                                    nonlocal prev_avd_last, prev_evav
                                    nbatch = ceil_div(t + 1, 4)
                                    ev_lm = {}
                                    tb = None
                                    for jb in range(nbatch):
                                        bw = min(4, (t + 1) - jb * 4)
                                        buf = (psT if (jb % 2 == 0 or
                                               not TBUF2) else psA[0])
                                        for j2 in range(bw):
                                            jt = jb * 4 + j2
                                            w = []
                                            if j2 == 0:
                                                bix = (jb % 2) if TBUF2 else 0
                                                w = [("DV", dd),
                                                     tbuf_free[bix]]
                                                if bix == 1 and psA0_extra:
                                                    w.append(psA0_extra.pop())
                                            tb = E.op(
                                                "tensor",
                                                lambda e, jt=jt, j2=j2, rp=rp,
                                                buf=buf: e.matmul(
                                                    buf[:, j2 * 128:
                                                        (j2 + 1) * 128],
                                                    at_sb[:, rp * N + jt * 128:
                                                          rp * N + (jt + 1) * 128],
                                                    d_sb[:, rp * 128:
                                                         (rp + 1) * 128],
                                                    start=True, stop=True),
                                                waits=w,
                                                inc=("PE" if j2 == bw - 1
                                                     else None))
                                        bix = (jb % 2) if TBUF2 else 0
                                        evt = E.op(
                                            "scalar",
                                            lambda e, jb=jb, bw=bw, buf=buf:
                                            e.activation(
                                                out=aT_sb[:, jb * 512:
                                                          jb * 512 + bw * 128],
                                                in_=buf[:, 0:bw * 128],
                                                func=AF.Identity),
                                            waits=[("PE", tb)], inc="AC")
                                        tbuf_free[bix] = ("AC", evt)
                                        ev_lm[jb] = ("AC", evt)
                                    first_av = True
                                    av = None
                                    for jb in range(nbatch):
                                        bw = min(4, (t + 1) - jb * 4)
                                        for j2 in range(bw):
                                            jt = jb * 4 + j2
                                            w = []
                                            if j2 == 0:
                                                w = [ev_lm[jb]]
                                                if first_av:
                                                    if hh % 2 == 0:
                                                        w.append(
                                                            ("AC", prev_evav))
                                                    first_av = False
                                            av = E.op(
                                                "tensor",
                                                lambda e, jt=jt, hh=hh, t=t,
                                                hb=hb: e.matmul(
                                                    psAV[hb:hb + 64, :],
                                                    v_sb[:, jt * D + hh * 64:
                                                         jt * D + (hh + 1) * 64],
                                                    aT_sb[:, jt * 128:
                                                          (jt + 1) * 128],
                                                    start=(jt == 0),
                                                    stop=(jt == t),
                                                    tile_position=(0, hb)),
                                                waits=w,
                                                inc=("PE" if jt == t else None))
                                    tb_last_par[rp] = tb
                                    prev_avd_last = av
                                    if hh % 2 == 1:
                                        prev_evav = E.op(
                                            "scalar",
                                            lambda e, hc=hc, tt=tt:
                                            e.activation(
                                                out=of_sb[:,
                                                          hc * 512 + tt * 128:
                                                          hc * 512 + (tt + 1) * 128],
                                                in_=psAV[:, :],
                                                func=AF.Identity),
                                            waits=[("PE", av)], inc="AC")
                                return stage_c

                            if old_c is not None:
                                old_c[0](old_c[1])
                            pend_b = mk_stage_b()
                            pend_cm = mk_stage_c()
                            if g == NG - 1 and s + 1 < SEQS and hh % 4 == 3:
                                ai = tt * 4 + hh // 4
                                if ai < NT:
                                    emit_a1(s + 1, ai)
                            rowc += 1
                    dd_last = pend_b()
                    if pend_c is not None:
                        pend_c[0](pend_c[1])
                    pend_cm(dd_last)
                    pend_b = None
                    pend_c = None
                    if TBUF2:
                        prev_psA_free[0] = tbuf_free[1]
                    if NREG == 5:
                        prev_psA_free[1] = ("AC", exp_region[4])

                    if s < 2:
                        emit_prep(3)
                    for dc in range(DCH):
                        bank = psA[dc % 2]
                        for jc in range(DCH):
                            w = []
                            if jc == 0:
                                w = [("AC", prev_evav),
                                     prev_psA_free.get(dc % 2, ("DV", 0))]
                            pj = E.op(
                                "tensor",
                                lambda e, dc=dc, jc=jc, bank=bank: e.matmul(
                                    bank[:, 0:512],
                                    wo_sb[:, jc * D + dc * 128:
                                          jc * D + (dc + 1) * 128],
                                    of_sb[:, jc * 512:(jc + 1) * 512],
                                    start=(jc == 0), stop=(jc == DCH - 1)),
                                waits=w, inc=("PE" if jc == DCH - 1 else None))
                        sl = dc % 2
                        ev = E.op("scalar",
                                  lambda e, sl=sl, bank=bank: e.activation(
                                      out=aoT_sb[:, sl * 512:(sl + 1) * 512],
                                      in_=bank[:, 0:512], func=AF.Identity),
                                  waits=[("PE", pj), ("GP", last_aost[sl])],
                                  inc="AC")
                        prev_psA_free[sl] = ("AC", ev)
                        last_aost[sl] = E.op(
                            "gpsimd",
                            lambda e, dc=dc, sl=sl, grow=grow: e.dma_start(
                                out=sc_ao[dc * 128:(dc + 1) * 128,
                                          grow:grow + 512],
                                in_=aoT_sb[:, sl * 512:(sl + 1) * 512]),
                            waits=[("AC", ev)], inc="GP", k=16)
                        if dc == DCH - 1:
                            prev_pj_done = pj
                            prev_aost = last_aost[sl]

                    for tt in range(4):
                        t = 4 * g + tt
                        row0 = s * N + t * 128
                        par = t % 2
                        tr = E.op("sync",
                                  lambda e, row0=row0, par=par: e.dma_start_transpose(
                                      out=aot_sb[:, par * D:(par + 1) * D],
                                      in_=sc_ao[:, row0:row0 + 128]),
                                  waits=[("GP", prev_aost),
                                         ("AC", pget(lm_lnA, s, t))],
                                  inc="HW", k=16)
                        stat, sg, bi = ln_stats(
                            lambda par=par: aot_sb[:, par * D:(par + 1) * D],
                            par, C1_LN, C2_LN, [("HW", tr)], base=64)
                        lnm = E.op("scalar",
                                   lambda e, par=par, sg=sg, bi=bi: e.activation(
                                       out=ln_sb[:, par * D:(par + 1) * D],
                                       in_=aot_sb[:, par * D:(par + 1) * D],
                                       func=AF.Identity, bias=bi, scale=sg),
                                   waits=[("DV", stat),
                                          ("GP", pget(lm_xmst, s, t))],
                                   inc="AC")
                        lm_lnA[(s, t)] = lnm
                        acc = E.op("gpsimd",
                                   lambda e, row0=row0, par=par: e.dma_start(
                                       out=ln_sb[:, par * D:(par + 1) * D],
                                       in_=x_h[row0:row0 + 128, :],
                                       accum_op=ALU.add),
                                   waits=[("AC", lnm)], inc="GP", k=16)
                        lm_xmst[(s, t)] = E.op(
                            "gpsimd",
                            lambda e, row0=row0, par=par: e.dma_start(
                                out=sc_xm[row0:row0 + 128, :],
                                in_=ln_sb[:, par * D:(par + 1) * D]),
                            waits=[("GP", acc)], inc="GP", k=16)

            emit_prep(len(prep_rounds))
            lm["ffnprep"] = E.cnt["GP"]

            E.op("sync", None, waits=[("HW", E.cnt["HW"])], inc="PH")
            E.op("gpsimd", None, waits=[("GP", E.cnt["GP"])], inc="PH")
            E.op("vector", None, inc="PH")
            E.op("scalar", None, inc="PH")
            E.op("tensor", None, inc="PH")
            lm["bar"] = E.cnt["PH"]

            with nc.Block() as blkA:
                E.replay(blkA)

        with ExitStack() as eb:
            wgT_sb = sb(eb, "wgT", [128, DCH * HID], BF16, is_transpose_dest=True)
            wuT_sb = sb(eb, "wuT", [128, DCH * HID], BF16, is_transpose_dest=True)
            wdT_sb = sb(eb, "wdT", [128, HCH * D], BF16, is_transpose_dest=True)
            ffm_sb = sb(eb, "ffm", [128, DCH * 512], BF16, is_transpose_dest=True)
            h_sb = sb(eb, "h", [128, HCH * 512], BF16)
            silu_sb = dump_sb
            psB = ps(eb, "psB", [128, 2048])
            psD = [ps(eb, "psD0", [128, 512]), ps(eb, "psD1", [128, 512])]

            bw_ = [("PH", lm["bar"])]
            for dc in range(DCH):
                h = E.op("sync",
                         lambda e, dc=dc: e.dma_start_transpose(
                             out=wgT_sb[:, dc * HID:(dc + 1) * HID],
                             in_=sc_wg[:, dc * 128:(dc + 1) * 128]),
                         waits=bw_ + [("GP", lm["ffnprep"])], inc="HW", k=16)
            for dc in range(DCH):
                h = E.op("sync",
                         lambda e, dc=dc: e.dma_start_transpose(
                             out=wuT_sb[:, dc * HID:(dc + 1) * HID],
                             in_=sc_wu[:, dc * 128:(dc + 1) * 128]),
                         inc="HW", k=16)
            for c in range(HCH):
                h = E.op("sync",
                         lambda e, c=c: e.dma_start_transpose(
                             out=wdT_sb[:, c * D:(c + 1) * D],
                             in_=sc_wd[:, c * 128:(c + 1) * 128]),
                         inc="HW", k=16)
            lm["wtB"] = h

            prev_b1_done = 0
            prev_dmm_last = 0
            prev_fost = 0
            last_fost = {0: 0, 1: 0}
            last_sil = {0: 0, 1: 0}
            last_hm = {0: 0, 1: 0}
            last_hm_any = 0
            last_evd = {0: 0, 1: 0}
            lm_fT = {}
            lm_lnB = {}
            lm_lnBe = {}
            lm_yst = {}
            first_b = True

            lm_ffm_map = {}

            def emit_fln(s, g):
                nonlocal first_b
                if True:
                    grow = s * N + g * 512
                    for tt in range(4):
                        t = 4 * g + tt
                        row0 = s * N + t * 128
                        par = t % 2
                        xsl = (slice(None), slice(par * D, (par + 1) * D))
                        w = [("AC", pget(lm_lnB, s, t))]
                        if first_b:
                            w = bw_ + w
                        ld = E.op("sync",
                                  lambda e, row0=row0, xsl=xsl: e.dma_start(
                                      out=xt_sb[xsl], in_=sc_xm[row0:row0 + 128, :]),
                                  waits=w, inc="HW", k=16)
                        wf = [("HW", ld)]
                        if first_b:
                            wf = bw_ + wf
                            first_b = False
                        stat, sg, bi = ln_stats(
                            lambda xsl=xsl: xt_sb[xsl], par, C1_LN, C2_LN, wf, base=32)
                        nrm = E.op("scalar",
                                   lambda e, xsl=xsl, par=par, sg=sg, bi=bi:
                                   e.activation(out=xp_sb[:, par * D:(par + 1) * D],
                                                in_=xt_sb[xsl], func=AF.Identity,
                                                bias=bi, scale=sg),
                                   waits=[("DV", stat),
                                          ("GP", pget(lm_fT, s, t))],
                                   inc="AC")
                        lm_lnB[(s, t)] = nrm
                        lm_fT[(s, t)] = E.op(
                            "gpsimd",
                            lambda e, row0=row0, par=par: e.dma_start(
                                out=sc_xp[row0:row0 + 128, :],
                                in_=xp_sb[:, par * D:(par + 1) * D]),
                            waits=[("AC", nrm)], inc="GP", k=16)

            def emit_ftr(s, g):
                if True:
                    grow = s * N + g * 512
                    for dc in range(DCH):
                        h = E.op("sync",
                                 lambda e, dc=dc, grow=grow: e.dma_start_transpose(
                                     out=ffm_sb[:, dc * 512:(dc + 1) * 512],
                                     in_=sc_xp[grow:grow + 512,
                                               dc * 128:(dc + 1) * 128]),
                                 waits=[("GP", lm_fT[(s, 4 * g + 3)]),
                                        ("PE", prev_b1_done)],
                                 inc="HW", k=16)
                    lm_ffm_map[(s, g)] = h


            def emit_b12(s, g, mid_hook=None):
                nonlocal prev_b1_done, prev_dmm_last, prev_fost, last_hm_any
                if True:
                    grow = s * N + g * 512
                    for c in range(HCH):
                        q0 = (c % 2) * 1024
                        for dc in range(DCH):
                            w = []
                            if dc == 0:
                                w = [("HW", lm_ffm_map[(s, g)]),
                                     ("HW", lm["wtB"]),
                                     ("AC", last_sil[c % 2])]
                            gmm = E.op(
                                "tensor",
                                lambda e, c=c, dc=dc, q0=q0: e.matmul(
                                    psB[:, q0:q0 + 512],
                                    wgT_sb[:, dc * HID + c * 128:
                                           dc * HID + (c + 1) * 128],
                                    ffm_sb[:, dc * 512:(dc + 1) * 512],
                                    start=(dc == 0), stop=(dc == DCH - 1)),
                                waits=w, inc=("PE" if dc == DCH - 1 else None))
                        for dc in range(DCH):
                            w = []
                            if dc == 0:
                                w = [("DV", last_hm[c % 2])]
                                if c == 0:
                                    w.append(("PE", prev_dmm_last))
                            umm = E.op(
                                "tensor",
                                lambda e, c=c, dc=dc, q0=q0: e.matmul(
                                    psB[:, q0 + 512:q0 + 1024],
                                    wuT_sb[:, dc * HID + c * 128:
                                           dc * HID + (c + 1) * 128],
                                    ffm_sb[:, dc * 512:(dc + 1) * 512],
                                    start=(dc == 0), stop=(dc == DCH - 1)),
                                waits=w, inc=("PE" if dc == DCH - 1 else None))
                        last_umm = umm
                        last_sil[c % 2] = E.op(
                            "scalar",
                            lambda e, q0=q0: e.activation(
                                out=silu_sb[:, 0:512], in_=psB[:, q0:q0 + 512],
                                func=AF.Silu, scale=DEQ),
                            waits=[("PE", gmm), ("DV", last_hm_any)],
                            inc="AC")
                        last_hm[c % 2] = last_hm_any = E.op(
                            "vector",
                            lambda e, c=c, q0=q0: e.tensor_tensor(
                                out=h_sb[:, c * 512:(c + 1) * 512],
                                in0=silu_sb[:, 0:512],
                                in1=psB[:, q0 + 512:q0 + 1024],
                                op=ALU.mult),
                            waits=[("AC", last_sil[c % 2]), ("PE", umm)],
                            inc="DV")
                        if c == 5 and mid_hook is not None:
                            mid_hook()
                    lm_hm_all = last_hm_any

                    for dc in range(DCH):
                        for c in range(HCH):
                            w = []
                            if c == 0:
                                w = [("DV", lm_hm_all),
                                     ("AC", last_evd[dc % 2])]
                            dmm = E.op(
                                "tensor",
                                lambda e, dc=dc, c=c: e.matmul(
                                    psD[dc % 2][:, 0:512],
                                    wdT_sb[:, c * D + dc * 128:
                                           c * D + (dc + 1) * 128],
                                    h_sb[:, c * 512:(c + 1) * 512],
                                    start=(c == 0), stop=(c == HCH - 1)),
                                waits=w, inc=("PE" if c == HCH - 1 else None))
                        sl = dc % 2
                        last_evd[sl] = E.op(
                            "scalar",
                            lambda e, sl=sl, dc=dc: e.activation(
                                out=aoT_sb[:, sl * 512:(sl + 1) * 512],
                                in_=psD[sl][:, 0:512], func=AF.Identity,
                                scale=DEQ2),
                            waits=[("PE", dmm), ("GP", last_fost[sl])],
                            inc="AC")
                        last_fost[sl] = E.op(
                            "gpsimd",
                            lambda e, dc=dc, sl=sl, grow=grow: e.dma_start(
                                out=sc_ao[dc * 128:(dc + 1) * 128,
                                          grow:grow + 512],
                                in_=aoT_sb[:, sl * 512:(sl + 1) * 512]),
                            waits=[("AC", last_evd[sl])], inc="GP", k=16)
                        if dc == DCH - 1:
                            prev_dmm_last = dmm
                            prev_fost = last_fost[sl]
                    prev_b1_done = last_umm


            def emit_epi(s, g):
                nonlocal prev_aost
                if True:
                    grow = s * N + g * 512
                    for tt in range(4):
                        t = 4 * g + tt
                        row0 = s * N + t * 128
                        par = t % 2
                        tr = E.op("sync",
                                  lambda e, row0=row0, par=par:
                                  e.dma_start_transpose(
                                      out=aot_sb[:, par * D:(par + 1) * D],
                                      in_=sc_ao[:, row0:row0 + 128]),
                                  waits=[("GP", prev_fost),
                                         ("AC", pget(lm_lnBe, s, t))],
                                  inc="HW", k=16)
                        stat, sg, bi = ln_stats(
                            lambda par=par: aot_sb[:, par * D:(par + 1) * D],
                            par, C1_LN, C2_LN, [("HW", tr)], base=64)
                        lnm = E.op("scalar",
                                   lambda e, par=par, sg=sg, bi=bi: e.activation(
                                       out=ln_sb[:, par * D:(par + 1) * D],
                                       in_=aot_sb[:, par * D:(par + 1) * D],
                                       func=AF.Identity, bias=bi, scale=sg),
                                   waits=[("DV", stat),
                                          ("GP", pget(lm_yst, s, t))],
                                   inc="AC")
                        lm_lnBe[(s, t)] = lnm
                        acc = E.op("gpsimd",
                                   lambda e, row0=row0, par=par: e.dma_start(
                                       out=ln_sb[:, par * D:(par + 1) * D],
                                       in_=sc_xm[row0:row0 + 128, :],
                                       accum_op=ALU.add),
                                   waits=[("AC", lnm)], inc="GP", k=16)
                        lm_yst[(s, t)] = E.op(
                            "gpsimd",
                            lambda e, row0=row0, par=par: e.dma_start(
                                out=y_h[row0:row0 + 128, :],
                                in_=ln_sb[:, par * D:(par + 1) * D]),
                            waits=[("GP", acc)], inc="GP", k=16)


            groups = [(s, g) for s in range(SEQS) for g in range(NG)]
            for pg in groups[:2]:
                emit_fln(*pg)
            emit_ftr(*groups[0])
            for gi, (s, g) in enumerate(groups):
                def mid_hook(gi=gi):
                    if gi + 2 < len(groups):
                        emit_fln(*groups[gi + 2])
                    if gi >= 1:
                        emit_epi(*groups[gi - 1])
                emit_b12(s, g, mid_hook)
                if gi + 1 < len(groups):
                    emit_ftr(*groups[gi + 1])
            emit_epi(*groups[-1])

            E.op("gpsimd", None, waits=[("GP", E.cnt["GP"])])

            with nc.Block() as blkB:
                E.replay(blkB)

    return nc


_NC_CACHE = {}


def kernel(L0_attn_tokens, L0_attn_norm_w, L0_attn_Wq, L0_attn_Wkv, L0_attn_Wout,
           L0_attn_Wg, L0_ffn_w_gate, L0_ffn_w_up, L0_ffn_w_down):
    if "nc" not in _NC_CACHE:
        _NC_CACHE["nc"] = build_program()
    nc = _NC_CACHE["nc"]

    toks = np.ascontiguousarray(L0_attn_tokens, dtype=np.float32)
    nw = np.ascontiguousarray(L0_attn_norm_w, dtype=np.float32).reshape(D, 1)
    wq = np.ascontiguousarray(L0_attn_Wq, dtype=np.float32)
    wkv = np.ascontiguousarray(L0_attn_Wkv, dtype=np.float32)
    wo = np.ascontiguousarray(L0_attn_Wout, dtype=np.float32)
    wg = np.ascontiguousarray(L0_attn_Wg, dtype=np.float32)
    wgi = np.ascontiguousarray(L0_ffn_w_gate, dtype=np.int32)
    wui = np.ascontiguousarray(L0_ffn_w_up, dtype=np.int32)
    wdi = np.ascontiguousarray(L0_ffn_w_down, dtype=np.int32)

    in_maps = []
    for c in range(8):
        shard = np.ascontiguousarray(
            toks[c * SEQS:(c + 1) * SEQS].reshape(ROWS, D))
        in_maps.append({
            "x": shard, "norm_w": nw, "Wq": wq, "Wkv": wkv, "Wout": wo,
            "Wg": wg, "w_gate": wgi, "w_up": wui, "w_down": wdi,
        })
    res = run_bass_kernel_spmd(nc, in_maps, core_ids=list(range(8)))
    out = np.concatenate(
        [r["y"].reshape(SEQS, N, D) for r in res.results], axis=0)
    return out


# version 27
# speedup vs baseline: 1.0378x; 1.0378x over previous
import math
from contextlib import ExitStack

import numpy as np
import ml_dtypes
import concourse.bass as bass
import concourse.mybir as mybir
from concourse.bass_utils import run_bass_kernel_spmd

F32 = mybir.dt.float32
BF16 = mybir.dt.bfloat16
I32 = mybir.dt.int32
AF = mybir.ActivationFunctionType
ALU = mybir.AluOpType
AX = mybir.AxisListType

SEQS = 4
N = 2048
D = 1024
NT = 16
NG = 4
HEADS = 16
DH = 64
HID = 3072
DCH = 8
HCH = 24
ROWS = SEQS * N

EPS_R = float(np.finfo(np.float32).eps)
C1_RMS = 1.0 + EPS_R
C2_RMS = 1e-5 * EPS_R
C1_LN = 1.0
C2_LN = 1e-5
QK_SCALE = 0.125
DEQ = 1.0 / 127.0
DEQ2 = 1.0 / (127.0 * 127.0)
NEGBIG = -1.0e30

ENGINES = ("sync", "gpsimd", "vector", "scalar", "tensor")
DEBUG_TAPS = False
ATTN_DEFER = True
NREG = 5
USE_BN = True
TBUF2 = True


class Emitter:
    def __init__(self, sems):
        self.sems = sems
        self.events = {k: [] for k in ENGINES}
        self.cnt = {k: 0 for k in sems}

    def op(self, engk, builder, waits=(), inc=None, k=1):
        tgt = None
        if inc is not None:
            self.cnt[inc] += k
            tgt = self.cnt[inc]
        w = tuple((s, v) for (s, v) in waits if v and v > 0)

        def fn(eng, _w=w, _b=builder, _i=inc, _k=k):
            for s, v in _w:
                eng.wait_ge(self.sems[s], v)
            r = _b(eng) if _b is not None else None
            if _i is not None:
                if r is None:
                    r = eng.nop()
                r.then_inc(self.sems[_i], _k)

        self.events[engk].append(fn)
        return tgt

    def replay(self, block):
        for key, reg in (("sync", block.sync), ("gpsimd", block.gpsimd),
                         ("vector", block.vector), ("scalar", block.scalar),
                         ("tensor", block.tensor)):
            evs = self.events[key]

            def body(eng, _evs=evs):
                for f in _evs:
                    f(eng)

            reg(body)
        self.events = {k: [] for k in ENGINES}


def ceil_div(a, b):
    return (a + b - 1) // b


def build_program():
    nc = bass.Bass("TRN2", target_bir_lowering=False, debug=False,
                   detect_race_conditions=False)

    x_h = nc.dram_tensor("x", [ROWS, D], F32, kind="ExternalInput").ap()
    nw_h = nc.dram_tensor("norm_w", [D, 1], F32, kind="ExternalInput").ap()
    wq_h = nc.dram_tensor("Wq", [D, D], F32, kind="ExternalInput").ap()
    wkv_h = nc.dram_tensor("Wkv", [D, 2 * D], F32, kind="ExternalInput").ap()
    wo_h = nc.dram_tensor("Wout", [D, D], F32, kind="ExternalInput").ap()
    wg_h = nc.dram_tensor("Wg", [D, HEADS], F32, kind="ExternalInput").ap()
    wgi_h = nc.dram_tensor("w_gate", [HID, D], I32, kind="ExternalInput").ap()
    wui_h = nc.dram_tensor("w_up", [HID, D], I32, kind="ExternalInput").ap()
    wdi_h = nc.dram_tensor("w_down", [D, HID], I32, kind="ExternalInput").ap()
    y_h = nc.dram_tensor("y", [ROWS, D], F32, kind="ExternalOutput").ap()

    _k = "ExternalOutput" if DEBUG_TAPS else "Internal"
    sc_xp = nc.dram_tensor("sc_xp", [ROWS, D], BF16, kind=_k).ap()
    sc_ao = nc.dram_tensor("sc_ao", [D, ROWS], BF16, kind=_k).ap()
    sc_xm = nc.dram_tensor("sc_xm", [ROWS, D], F32, kind=_k).ap()
    sc_wg = nc.dram_tensor("sc_wg", [HID, D], BF16, kind="Internal").ap()
    sc_wu = nc.dram_tensor("sc_wu", [HID, D], BF16, kind="Internal").ap()
    sc_wd = nc.dram_tensor("sc_wd", [D, HID], BF16, kind="Internal").ap()

    ident_c = nc.inline_tensor(np.eye(128, dtype=ml_dtypes.bfloat16), name="identc")
    maskt_c = nc.inline_tensor(
        np.tril(np.full((128, 128), NEGBIG, np.float32), -1).astype(ml_dtypes.bfloat16),
        name="masktc")

    with ExitStack() as es:
        sems = {}
        for snm in ("HW", "GP", "PE", "DV", "AC", "PH", "PL"):
            sems[snm] = es.enter_context(nc.semaphore("s" + snm))
        E = Emitter(sems)
        lm = {}

        def sb(stack, name, shape, dtype, **kw):
            return stack.enter_context(nc.sbuf_tensor(name, shape, dtype, **kw))

        def ps(stack, name, shape):
            return stack.enter_context(nc.psum_tensor(name, shape, F32))

        xt_sb = sb(es, "xt", [128, 2 * D], F32)
        xp_sb = sb(es, "xp", [128, 2 * D], BF16)
        dump_sb = sb(es, "dump", [128, D], BF16)
        dump2_sb = (sb(es, "dump2", [128, D], BF16)
                    if not USE_BN else dump_sb)
        ln_sb = sb(es, "ln", [128, 2 * D], F32)
        aot_sb = sb(es, "aot", [128, 2 * D], BF16, is_transpose_dest=True)
        aoT_sb = sb(es, "aoT", [128, 2 * 512], BF16)
        st_sb = sb(es, "stats", [128, 128], F32)

        def stc(idx):
            return st_sb[:, idx:idx + 1]

        def st2(a, b):
            return st_sb[:, a:b]


        def pget(d, s, t):
            if t >= 2:
                return d.get((s, t - 2), 0)
            if s > 0:
                return d.get((s - 1, NT + t - 2), 0)
            return 0

        with ExitStack() as ea:
            wq_sb = sb(ea, "wq", [128, DCH * D], BF16)
            wkv_sb = sb(ea, "wkv", [128, DCH * 2 * D], BF16)
            wo_sb = sb(ea, "wo", [128, DCH * D], BF16)
            wg_sb = sb(ea, "wgt", [128, DCH * HEADS], BF16)
            nw_sb = sb(ea, "nw", [128, DCH], F32)
            id_sb = sb(ea, "id", [128, 128], BF16)
            mk_sb = sb(ea, "mk", [128, 128], BF16)
            stage_sb = sb(ea, "stage", [128, 3072], BF16)
            xfm_sb = sb(ea, "xfm", [128, DCH * 512], BF16, is_transpose_dest=True)
            qT_sb = sb(ea, "qT", [128, DCH * 512], BF16)
            kT_sb = sb(ea, "kT", [128, DCH * N], BF16)
            v_sb = sb(ea, "v", [128, NT * D], BF16)
            ga_sb = sb(ea, "ga", [128, NT * HEADS], F32)
            at_sb = sb(ea, "at", [128, 2 * N], BF16)
            aT_sb = sb(ea, "aT", [128, N], BF16)
            d_sb = sb(ea, "d", [128, 2 * 128], BF16)
            of_sb = sb(ea, "of", [128, DCH * 512], BF16)

            psR = [ps(ea, "psR0", [128, 512]), ps(ea, "psR1", [128, 512]),
                   ps(ea, "psR2", [128, 512]), ps(ea, "psR3", [128, 512])]
            psT = ps(ea, "psT", [128, 512])
            psAV = ps(ea, "psAV", [128, 128])
            psA = [ps(ea, "psA0", [128, 512]), ps(ea, "psA1", [128, 512])]
            sreg = psR + [psA[1]]

            E.op("sync", lambda e: e.dma_start(out=id_sb[:], in_=ident_c.ap()[:, :]),
                 inc="HW", k=16)
            E.op("sync", lambda e: e.dma_start(out=mk_sb[:], in_=maskt_c.ap()[:, :]),
                 inc="HW", k=16)
            for dc in range(DCH):
                E.op("sync",
                     lambda e, dc=dc: e.dma_start(
                         out=nw_sb[:, dc:dc + 1], in_=nw_h[dc * 128:(dc + 1) * 128, 0:1]),
                     inc="HW", k=16)
            lm["consts"] = E.cnt["HW"]

            E.op("gpsimd", lambda e: e.dma_start(
                out=wq_sb.ap().rearrange("p (c j) -> p c j", c=DCH),
                in_=wq_h.rearrange("(c p) j -> p c j", p=128)), inc="GP", k=16)
            E.op("gpsimd", lambda e: e.dma_start(
                out=wkv_sb.ap().rearrange("p (c j) -> p c j", c=DCH),
                in_=wkv_h.rearrange("(c p) j -> p c j", p=128)), inc="GP", k=16)
            E.op("gpsimd", lambda e: e.dma_start(
                out=wo_sb.ap().rearrange("p (c j) -> p c j", c=DCH),
                in_=wo_h.rearrange("(c p) j -> p c j", p=128)), inc="GP", k=16)
            E.op("gpsimd", lambda e: e.dma_start(
                out=wg_sb.ap().rearrange("p (c j) -> p c j", c=DCH),
                in_=wg_h.rearrange("(c p) j -> p c j", p=128)), inc="GP", k=16)
            lm["wcast"] = E.cnt["GP"]

            for dc in range(DCH):
                E.op("vector",
                     lambda e, dc=dc: e.tensor_scalar(
                         out=wq_sb[:, dc * D:(dc + 1) * D],
                         in0=wq_sb[:, dc * D:(dc + 1) * D],
                         scalar1=nw_sb[:, dc:dc + 1], scalar2=None, op0=ALU.mult),
                     waits=[("GP", lm["wcast"]), ("HW", lm["consts"])])
            for dc in range(DCH):
                E.op("vector",
                     lambda e, dc=dc: e.tensor_scalar(
                         out=wkv_sb[:, dc * 2 * D:(dc + 1) * 2 * D],
                         in0=wkv_sb[:, dc * 2 * D:(dc + 1) * 2 * D],
                         scalar1=nw_sb[:, dc:dc + 1], scalar2=None, op0=ALU.mult))
            for dc in range(DCH):
                E.op("vector",
                     lambda e, dc=dc: e.tensor_scalar(
                         out=wg_sb[:, dc * HEADS:(dc + 1) * HEADS],
                         in0=wg_sb[:, dc * HEADS:(dc + 1) * HEADS],
                         scalar1=nw_sb[:, dc:dc + 1], scalar2=None, op0=ALU.mult),
                     inc=("DV" if dc == DCH - 1 else None))
            lm["wscaled"] = E.cnt["DV"]

            prep_state = {"prev_store": 0}
            prep_rounds = []
            for srcw, dstw, nrounds, rows in ((wgi_h, sc_wg, 8, 384),
                                              (wui_h, sc_wu, 8, 384),
                                              (wdi_h, sc_wd, 8, 128)):
                for r in range(nrounds):
                    prep_rounds.append((srcw, dstw, r, rows))

            def emit_prep(k):
                while k > 0 and prep_rounds:
                    srcw, dstw, r, rows = prep_rounds.pop(0)
                    if rows == 384:
                        cast = E.op("gpsimd",
                                    lambda e, srcw=srcw, r=r: e.dma_start(
                                        out=stage_sb.ap().rearrange(
                                            "p (c d) -> p c d", c=3),
                                        in_=srcw[r * 384:(r + 1) * 384, :].rearrange(
                                            "(c p) d -> p c d", p=128)),
                                    waits=[("GP", prep_state["prev_store"])],
                                    inc="GP", k=16)
                        prep_state["prev_store"] = E.op(
                            "gpsimd",
                            lambda e, dstw=dstw, r=r: e.dma_start(
                                out=dstw[r * 384:(r + 1) * 384, :].rearrange(
                                    "(c p) d -> p c d", p=128),
                                in_=stage_sb.ap().rearrange(
                                    "p (c d) -> p c d", c=3)),
                            waits=[("GP", cast)], inc="GP", k=16)
                    else:
                        cast = E.op("gpsimd",
                                    lambda e, srcw=srcw, r=r: e.dma_start(
                                        out=stage_sb[:, 0:HID],
                                        in_=srcw[r * 128:(r + 1) * 128, :]),
                                    waits=[("GP", prep_state["prev_store"])],
                                    inc="GP", k=16)
                        prep_state["prev_store"] = E.op(
                            "gpsimd",
                            lambda e, dstw=dstw, r=r: e.dma_start(
                                out=dstw[r * 128:(r + 1) * 128, :],
                                in_=stage_sb[:, 0:HID]),
                            waits=[("GP", cast)], inc="GP", k=16)
                    k -= 1

            def ln_stats(src_ap_fn, par, c1, c2, waits_first, base=0):
                if USE_BN:
                    return ln_stats_bn(src_ap_fn, par, c1, c2, waits_first, base)
                return ln_stats_sq(src_ap_fn, par, c1, c2, waits_first, base)

            def ln_stats_bn(src_ap_fn, par, c1, c2, waits_first, base=0):
                b = base + par * 16
                v1 = E.op("vector",
                          lambda e: (e.bn_stats(out=st2(b, b + 6),
                                                in_=src_ap_fn()[:, 0:512]),
                                     e.bn_stats(out=st2(b + 6, b + 12),
                                                in_=src_ap_fn()[:, 512:1024]))[-1],
                          waits=waits_first, inc="DV")
                v2 = E.op("vector",
                          lambda e: e.bn_aggr(out=st2(b + 12, b + 14),
                                              in_=st2(b, b + 12)),
                          waits=[("DV", v1)], inc="DV")
                v3 = E.op("vector",
                          lambda e: e.tensor_scalar(out=stc(b + 13),
                                                    in0=stc(b + 13),
                                                    scalar1=c1, scalar2=c2,
                                                    op0=ALU.mult, op1=ALU.add),
                          waits=[("DV", v2)], inc="DV")
                a2 = E.op("scalar",
                          lambda e: e.activation(out=stc(b + 14), in_=stc(b + 13),
                                                 func=AF.Ln),
                          waits=[("DV", v3)], inc="AC")
                a3 = E.op("scalar",
                          lambda e: e.activation(out=stc(b + 14), in_=stc(b + 14),
                                                 func=AF.Exp, scale=-0.5),
                          waits=[("AC", a2)], inc="AC")
                stat = E.op("vector",
                            lambda e: e.tensor_scalar(out=stc(b + 15),
                                                      in0=stc(b + 12),
                                                      scalar1=stc(b + 14),
                                                      scalar2=-1.0,
                                                      op0=ALU.mult,
                                                      op1=ALU.mult),
                            waits=[("AC", a3)], inc="DV")
                return stat, stc(b + 14), stc(b + 15)

            def ln_stats_sq(src_ap_fn, par, c1, c2, waits_first, base=0):
                b = base + par * 8
                sq = E.op("scalar",
                          lambda e: e.activation(out=dump2_sb[:, :],
                                                 in_=src_ap_fn(),
                                                 func=AF.Square,
                                                 accum_out=stc(b + 1)),
                          waits=waits_first, inc="AC")
                v1 = E.op("vector",
                          lambda e: e.reduce_sum(out=stc(b + 0), in_=src_ap_fn(),
                                                 axis=AX.X, negate=True),
                          waits=waits_first, inc="DV")
                v2 = E.op("vector",
                          lambda e: e.tensor_scalar(out=stc(b + 2), in0=stc(b + 1),
                                                    scalar1=1.0 / D, scalar2=None,
                                                    op0=ALU.mult),
                          waits=[("AC", sq)], inc="DV")
                v3 = E.op("vector",
                          lambda e: e.tensor_scalar(out=stc(b + 3), in0=stc(b + 0),
                                                    scalar1=stc(b + 0),
                                                    scalar2=1.0 / (D * D),
                                                    op0=ALU.mult, op1=ALU.mult),
                          waits=[("DV", v1)], inc="DV")
                v4 = E.op("vector",
                          lambda e: e.tensor_tensor(out=stc(b + 2), in0=stc(b + 2),
                                                    in1=stc(b + 3), op=ALU.subtract),
                          waits=[("DV", v3)], inc="DV")
                v5 = E.op("vector",
                          lambda e: e.tensor_scalar(out=stc(b + 2), in0=stc(b + 2),
                                                    scalar1=c1, scalar2=c2,
                                                    op0=ALU.mult, op1=ALU.add),
                          waits=[("DV", v4)], inc="DV")
                a2 = E.op("scalar",
                          lambda e: e.activation(out=stc(b + 4), in_=stc(b + 2),
                                                 func=AF.Ln),
                          waits=[("DV", v5)], inc="AC")
                a3 = E.op("scalar",
                          lambda e: e.activation(out=stc(b + 4), in_=stc(b + 4),
                                                 func=AF.Exp, scale=-0.5),
                          waits=[("AC", a2)], inc="AC")
                stat = E.op("vector",
                            lambda e: e.tensor_scalar(out=stc(b + 5), in0=stc(b + 0),
                                                      scalar1=stc(b + 4),
                                                      scalar2=1.0 / D,
                                                      op0=ALU.mult,
                                                      op1=ALU.mult),
                            waits=[("AC", a3)], inc="DV")
                return stat, stc(b + 4), stc(b + 5)

            prev_qkv_done = 0
            prev_avd_last = 0
            prev_dd_last = 0
            exp_region = [0, 0, 0, 0, 0]
            tb_last_par = {0: 0, 1: 0}
            u4 = 0
            rowc = 0
            evt_alt = 0
            prev_evav = 0
            tbuf_free = {0: ("DV", 0), 1: ("DV", 0)}
            prev_pj_done = 0
            prev_aost = 0
            last_aost = {0: 0, 1: 0}
            prev_psA_free = {}
            lm_norm = {}
            lm_xpst = {}
            lm_lnA = {}
            lm_xmst = {}
            pend_epiA = None

            def emit_a1(s, t):
                if True:
                    row0 = s * N + t * 128
                    par = t % 2
                    xsl = (slice(None), slice(par * D, (par + 1) * D))
                    ld = E.op("sync",
                              lambda e, row0=row0, xsl=xsl: e.dma_start(
                                  out=xt_sb[xsl], in_=x_h[row0:row0 + 128, :]),
                              waits=[("AC", pget(lm_norm, s, t))],
                              inc="HW", k=16)
                    stat, sg, bi = ln_stats(
                        lambda xsl=xsl: xt_sb[xsl], par, C1_RMS, C2_RMS,
                        [("HW", ld)], base=32)
                    nrm = E.op("scalar",
                               lambda e, xsl=xsl, par=par, sg=sg, bi=bi:
                               e.activation(out=xp_sb[:, par * D:(par + 1) * D],
                                            in_=xt_sb[xsl], func=AF.Identity,
                                            bias=bi, scale=sg),
                               waits=[("DV", stat),
                                      ("GP", pget(lm_xpst, s, t))],
                               inc="AC")
                    lm_norm[(s, t)] = nrm
                    lm_xpst[(s, t)] = E.op(
                        "gpsimd",
                        lambda e, row0=row0, par=par: e.dma_start(
                            out=sc_xp[row0:row0 + 128, :],
                            in_=xp_sb[:, par * D:(par + 1) * D]),
                        waits=[("AC", nrm)], inc="GP", k=16)

            for t in range(NT):
                emit_a1(0, t)
            for s in range(SEQS):
                for g in range(NG):
                    grow = s * N + g * 512
                    for dc in range(DCH):
                        h = E.op("sync",
                                 lambda e, dc=dc, grow=grow: e.dma_start_transpose(
                                     out=xfm_sb[:, dc * 512:(dc + 1) * 512],
                                     in_=sc_xp[grow:grow + 512,
                                               dc * 128:(dc + 1) * 128]),
                                 waits=[("GP", lm_xpst[(s, 4 * g + 3)]),
                                        ("PE", prev_qkv_done)],
                                 inc="HW", k=16)
                    lm_xfm = h

                    if pend_epiA is not None and False:
                        pass
                    units = ([("q", hc) for hc in range(DCH)]
                             + [("k", hc) for hc in range(DCH)]
                             + [("v", tt, nn) for tt in range(4) for nn in range(2)]
                             + [("g", tt) for tt in range(4)])
                    for ui, u in enumerate(units):
                        bank = psA[ui % 2]
                        free_w = [prev_psA_free.get(ui % 2, ("DV", 0))]
                        for dc in range(DCH):
                            if u[0] == "q":
                                lhsT = wq_sb[:, dc * D + u[1] * 128:
                                             dc * D + (u[1] + 1) * 128]
                                rhs = xfm_sb[:, dc * 512:(dc + 1) * 512]
                                oap = bank[:, 0:512]
                            elif u[0] == "k":
                                lhsT = wkv_sb[:, dc * 2 * D + u[1] * 128:
                                              dc * 2 * D + (u[1] + 1) * 128]
                                rhs = xfm_sb[:, dc * 512:(dc + 1) * 512]
                                oap = bank[:, 0:512]
                            elif u[0] == "v":
                                tt, nn = u[1], u[2]
                                lhsT = xfm_sb[:, dc * 512 + tt * 128:
                                              dc * 512 + (tt + 1) * 128]
                                rhs = wkv_sb[:, dc * 2 * D + D + nn * 512:
                                             dc * 2 * D + D + (nn + 1) * 512]
                                oap = bank[:, 0:512]
                            else:
                                tt = u[1]
                                lhsT = xfm_sb[:, dc * 512 + tt * 128:
                                              dc * 512 + (tt + 1) * 128]
                                rhs = wg_sb[:, dc * HEADS:(dc + 1) * HEADS]
                                oap = bank[:, 0:HEADS]
                            w = ([("HW", lm_xfm), ("DV", lm["wscaled"])]
                                 + [(sm, v) for (sm, v) in free_w]
                                 ) if dc == 0 else []
                            mmlm = E.op("tensor",
                                        lambda e, o=oap, l=lhsT, r=rhs, dc=dc:
                                        e.matmul(o, l, r, start=(dc == 0),
                                                 stop=(dc == DCH - 1)),
                                        waits=w,
                                        inc=("PE" if dc == DCH - 1 else None))
                        ev_eng = "scalar" if (u[0] == "g" or ui % 2 == 1) else "vector"
                        if u[0] == "q":
                            dst = qT_sb[:, u[1] * 512:(u[1] + 1) * 512]
                        elif u[0] == "k":
                            dst = kT_sb[:, u[1] * N + g * 512:
                                        u[1] * N + (g + 1) * 512]
                        elif u[0] == "v":
                            tt, nn = u[1], u[2]
                            dst = v_sb[:, (4 * g + tt) * D + nn * 512:
                                       (4 * g + tt) * D + (nn + 1) * 512]
                        else:
                            tt = u[1]
                            dst = ga_sb[:, (4 * g + tt) * HEADS:
                                        (4 * g + tt + 1) * HEADS]
                        wz = [("PE", mmlm), ("PE", prev_avd_last)]
                        if u[0] == "g":
                            wz.append(("DV", prev_dd_last))
                            evlm = E.op("scalar",
                                        lambda e, dst=dst, bank=bank: e.activation(
                                            out=dst, in_=bank[:, 0:HEADS],
                                            func=AF.Exp, scale=-1.0),
                                        waits=wz, inc="AC")
                            g1 = E.op("vector",
                                      lambda e, dst=dst: e.tensor_scalar(
                                          out=dst, in0=dst, scalar1=1.0,
                                          scalar2=None, op0=ALU.add),
                                      waits=[("AC", evlm)], inc="DV")
                            E.op("vector",
                                 lambda e, dst=dst: e.reciprocal(
                                     out=dst, in_=dst),
                                 waits=[("DV", g1)], inc="DV")
                            prev_psA_free[ui % 2] = ("AC", evlm)
                        else:
                            if ev_eng == "vector":
                                evlm = E.op("vector",
                                            lambda e, dst=dst, bank=bank:
                                            e.tensor_copy(out=dst,
                                                          in_=bank[:, 0:512]),
                                            waits=wz, inc="DV")
                                prev_psA_free[ui % 2] = ("DV", evlm)
                            else:
                                evlm = E.op("scalar",
                                            lambda e, dst=dst, bank=bank:
                                            e.activation(out=dst,
                                                         in_=bank[:, 0:512],
                                                         func=AF.Identity),
                                            waits=wz, inc="AC")
                                prev_psA_free[ui % 2] = ("AC", evlm)
                        if ui == 11 and pend_epiA is not None:
                            pend_epiA()
                            pend_epiA = None
                        if ui == len(units) - 1:
                            prev_qkv_done = mmlm

                    pend_b = None
                    pend_cm = None
                    pend_c = None
                    psA0_extra = [prev_psA_free.get(0, ("DV", 0))]
                    psA1_extra = [prev_psA_free.get(1, ("DV", 0))]
                    for tt in range(4):
                        t = 4 * g + tt
                        for hh in range(HEADS):
                            new_c = None
                            if pend_b is not None:
                                dd_prev = pend_b()
                                new_c = (pend_cm, dd_prev)
                            old_c = pend_c
                            pend_c = new_c
                            L = (t + 1) * 128
                            rp = rowc % 2
                            hb = (hh % 2) * 64
                            hc = hh // 2
                            qsl = (slice(hb, hb + 64),
                                   slice(hc * 512 + tt * 128,
                                         hc * 512 + (tt + 1) * 128))
                            nu = ceil_div(L, 512)
                            uinf = []
                            for ui in range(nu):
                                c_lo = ui * 512
                                c_hi = min(L, c_lo + 512)
                                ri = u4 % NREG
                                Lu = c_hi - c_lo
                                nmc = 0 + rp * 4 + ui
                                rsc = 16 + rp * 4 + ui
                                has_diag = (c_hi == L)
                                fw = [("AC", exp_region[ri])]
                                if ri == 4 and psA1_extra:
                                    fw.append(psA1_extra.pop())
                                if has_diag:
                                    if Lu > 128:
                                        E.op("tensor",
                                             lambda e, ri=ri, Lu=Lu, c_lo=c_lo,
                                             qsl=qsl, hb=hb, hc=hc: e.matmul(
                                                 sreg[ri][:, 0:Lu - 128],
                                                 qT_sb[qsl],
                                                 kT_sb[hb:hb + 64,
                                                       hc * N + c_lo:
                                                       hc * N + c_lo + Lu - 128],
                                                 start=True, stop=True),
                                             waits=fw)
                                        fw = []
                                    E.op("tensor",
                                         lambda e, ri=ri, Lu=Lu: e.matmul(
                                             sreg[ri][:, Lu - 128:Lu],
                                             mk_sb[:], id_sb[:],
                                             start=True, stop=False),
                                         waits=fw)
                                    smm = E.op(
                                        "tensor",
                                        lambda e, ri=ri, Lu=Lu, c_lo=c_lo,
                                        qsl=qsl, hb=hb, hc=hc: e.matmul(
                                            sreg[ri][:, Lu - 128:Lu],
                                            qT_sb[qsl],
                                            kT_sb[hb:hb + 64,
                                                  hc * N + c_lo + Lu - 128:
                                                  hc * N + c_lo + Lu],
                                            start=False, stop=True),
                                        inc="PE")
                                else:
                                    smm = E.op(
                                        "tensor",
                                        lambda e, ri=ri, Lu=Lu, c_lo=c_lo,
                                        qsl=qsl, hb=hb, hc=hc: e.matmul(
                                            sreg[ri][:, 0:Lu], qT_sb[qsl],
                                            kT_sb[hb:hb + 64,
                                                  hc * N + c_lo:hc * N + c_lo + Lu],
                                            start=True, stop=True),
                                        waits=fw, inc="PE")
                                mxl = E.op(
                                    "vector",
                                    lambda e, ri=ri, Lu=Lu, nmc=nmc:
                                    e.reduce_max(out=stc(nmc),
                                                 in_=sreg[ri][:, 0:Lu],
                                                 axis=AX.X, negate=True),
                                    waits=[("PE", smm)], inc="DV")
                                uinf.append((ri, nmc, rsc, c_lo, c_hi, mxl))
                                u4 += 1
                            bc = 8 + rp
                            sc_ = 10 + rp
                            cc_ = 12 + rp
                            if nu == 1:
                                bias_lm = E.op(
                                    "vector",
                                    lambda e, u=uinf[0], bc=bc: e.tensor_scalar(
                                        out=stc(bc), in0=stc(u[1]),
                                        scalar1=QK_SCALE, scalar2=None,
                                        op0=ALU.mult),
                                    waits=[("DV", uinf[0][5])], inc="DV")
                            else:
                                t0 = E.op(
                                    "vector",
                                    lambda e, rp=rp, nu=nu, bc=bc:
                                    e.tensor_reduce(
                                        out=stc(bc),
                                        in_=st_sb[:, 0 + rp * 4:
                                                  0 + rp * 4 + nu],
                                        axis=AX.X, op=ALU.min),
                                    waits=[("DV", uinf[-1][5])], inc="DV")
                                bias_lm = E.op(
                                    "vector",
                                    lambda e, bc=bc: e.tensor_scalar(
                                        out=stc(bc), in0=stc(bc),
                                        scalar1=QK_SCALE, scalar2=None,
                                        op0=ALU.mult),
                                    waits=[("DV", t0)], inc="DV")

                            def mk_stage_b(rp=rp, nu=nu, uinf=uinf, bc=bc,
                                           sc_=sc_, cc_=cc_, bias_lm=bias_lm,
                                           t=t, hh=hh):
                                def stage_b():
                                    last_exp = 0
                                    for (ri, nmc, rsc, c_lo, c_hi, _) in uinf:
                                        last_exp = E.op(
                                            "scalar",
                                            lambda e, ri=ri, rsc=rsc, c_lo=c_lo,
                                            c_hi=c_hi, rp=rp, bc=bc:
                                            e.activation(
                                                out=at_sb[:, rp * N + c_lo:
                                                          rp * N + c_hi],
                                                in_=sreg[ri][:, 0:c_hi - c_lo],
                                                func=AF.Exp, scale=QK_SCALE,
                                                bias=stc(bc),
                                                accum_out=stc(rsc)),
                                            waits=[("DV", bias_lm),
                                                   ("PE", tb_last_par[rp])],
                                            inc="AC")
                                        exp_region[ri] = last_exp
                                    if nu > 1:
                                        s0 = E.op(
                                            "vector",
                                            lambda e, rp=rp, nu=nu, sc_=sc_:
                                            e.reduce_sum(
                                                out=stc(sc_),
                                                in_=st_sb[:, 16 + rp * 4:
                                                          16 + rp * 4 + nu],
                                                axis=AX.X),
                                            waits=[("AC", last_exp)], inc="DV")
                                        r0 = E.op(
                                            "vector",
                                            lambda e, sc_=sc_, cc_=cc_:
                                            e.reciprocal(
                                                out=stc(cc_), in_=stc(sc_)),
                                            waits=[("DV", s0)], inc="DV")
                                    else:
                                        r0 = E.op(
                                            "vector",
                                            lambda e, u0=uinf[0], cc_=cc_:
                                            e.reciprocal(
                                                out=stc(cc_), in_=stc(u0[2])),
                                            waits=[("AC", last_exp)], inc="DV")
                                    dd = E.op(
                                        "vector",
                                        lambda e, cc_=cc_, rp=rp, t=t, hh=hh:
                                        e.tensor_scalar(
                                            out=d_sb[:, rp * 128:(rp + 1) * 128],
                                            in0=id_sb[:], scalar1=stc(cc_),
                                            scalar2=ga_sb[:, t * HEADS + hh:
                                                          t * HEADS + hh + 1],
                                            op0=ALU.mult, op1=ALU.mult),
                                        waits=[("DV", r0),
                                               ("PE", tb_last_par[rp])],
                                        inc="DV")
                                    return dd
                                return stage_b

                            def mk_stage_c(t=t, hh=hh, hb=hb, hc=hc, tt=tt,
                                           rp=rp):
                                def stage_c(dd):
                                    nonlocal prev_avd_last, prev_evav
                                    nbatch = ceil_div(t + 1, 4)
                                    ev_lm = {}
                                    tb = None
                                    for jb in range(nbatch):
                                        bw = min(4, (t + 1) - jb * 4)
                                        buf = (psT if (jb % 2 == 0 or
                                               not TBUF2) else psA[0])
                                        for j2 in range(bw):
                                            jt = jb * 4 + j2
                                            w = []
                                            if j2 == 0:
                                                bix = (jb % 2) if TBUF2 else 0
                                                w = [("DV", dd),
                                                     tbuf_free[bix]]
                                                if bix == 1 and psA0_extra:
                                                    w.append(psA0_extra.pop())
                                            tb = E.op(
                                                "tensor",
                                                lambda e, jt=jt, j2=j2, rp=rp,
                                                buf=buf: e.matmul(
                                                    buf[:, j2 * 128:
                                                        (j2 + 1) * 128],
                                                    at_sb[:, rp * N + jt * 128:
                                                          rp * N + (jt + 1) * 128],
                                                    d_sb[:, rp * 128:
                                                         (rp + 1) * 128],
                                                    start=True, stop=True),
                                                waits=w,
                                                inc=("PE" if j2 == bw - 1
                                                     else None))
                                        bix = (jb % 2) if TBUF2 else 0
                                        evt = E.op(
                                            "scalar",
                                            lambda e, jb=jb, bw=bw, buf=buf:
                                            e.activation(
                                                out=aT_sb[:, jb * 512:
                                                          jb * 512 + bw * 128],
                                                in_=buf[:, 0:bw * 128],
                                                func=AF.Identity),
                                            waits=[("PE", tb)], inc="AC")
                                        tbuf_free[bix] = ("AC", evt)
                                        ev_lm[jb] = ("AC", evt)
                                    first_av = True
                                    av = None
                                    for jb in range(nbatch):
                                        bw = min(4, (t + 1) - jb * 4)
                                        for j2 in range(bw):
                                            jt = jb * 4 + j2
                                            w = []
                                            if j2 == 0:
                                                w = [ev_lm[jb]]
                                                if first_av:
                                                    if hh % 2 == 0:
                                                        w.append(
                                                            ("AC", prev_evav))
                                                    first_av = False
                                            av = E.op(
                                                "tensor",
                                                lambda e, jt=jt, hh=hh, t=t,
                                                hb=hb: e.matmul(
                                                    psAV[hb:hb + 64, :],
                                                    v_sb[:, jt * D + hh * 64:
                                                         jt * D + (hh + 1) * 64],
                                                    aT_sb[:, jt * 128:
                                                          (jt + 1) * 128],
                                                    start=(jt == 0),
                                                    stop=(jt == t),
                                                    tile_position=(0, hb)),
                                                waits=w,
                                                inc=("PE" if jt == t else None))
                                    tb_last_par[rp] = tb
                                    prev_avd_last = av
                                    if hh % 2 == 1:
                                        prev_evav = E.op(
                                            "scalar",
                                            lambda e, hc=hc, tt=tt:
                                            e.activation(
                                                out=of_sb[:,
                                                          hc * 512 + tt * 128:
                                                          hc * 512 + (tt + 1) * 128],
                                                in_=psAV[:, :],
                                                func=AF.Identity),
                                            waits=[("PE", av)], inc="AC")
                                return stage_c

                            if old_c is not None:
                                old_c[0](old_c[1])
                            pend_b = mk_stage_b()
                            pend_cm = mk_stage_c()
                            if g == NG - 1 and s + 1 < SEQS and hh % 4 == 3:
                                ai = tt * 4 + hh // 4
                                if ai < NT:
                                    emit_a1(s + 1, ai)
                            rowc += 1
                    dd_last = pend_b()
                    if pend_c is not None:
                        pend_c[0](pend_c[1])
                    pend_cm(dd_last)
                    pend_b = None
                    pend_c = None
                    if TBUF2:
                        prev_psA_free[0] = tbuf_free[1]
                    if NREG == 5:
                        prev_psA_free[1] = ("AC", exp_region[4])

                    if s < 2:
                        emit_prep(3)
                    for dc in range(DCH):
                        bank = psA[dc % 2]
                        for jc in range(DCH):
                            w = []
                            if jc == 0:
                                w = [("AC", prev_evav),
                                     prev_psA_free.get(dc % 2, ("DV", 0))]
                            pj = E.op(
                                "tensor",
                                lambda e, dc=dc, jc=jc, bank=bank: e.matmul(
                                    bank[:, 0:512],
                                    wo_sb[:, jc * D + dc * 128:
                                          jc * D + (dc + 1) * 128],
                                    of_sb[:, jc * 512:(jc + 1) * 512],
                                    start=(jc == 0), stop=(jc == DCH - 1)),
                                waits=w, inc=("PE" if jc == DCH - 1 else None))
                        sl = dc % 2
                        ev = E.op("scalar",
                                  lambda e, sl=sl, bank=bank: e.activation(
                                      out=aoT_sb[:, sl * 512:(sl + 1) * 512],
                                      in_=bank[:, 0:512], func=AF.Identity),
                                  waits=[("PE", pj), ("GP", last_aost[sl])],
                                  inc="AC")
                        prev_psA_free[sl] = ("AC", ev)
                        last_aost[sl] = E.op(
                            "gpsimd",
                            lambda e, dc=dc, sl=sl, grow=grow: e.dma_start(
                                out=sc_ao[dc * 128:(dc + 1) * 128,
                                          grow:grow + 512],
                                in_=aoT_sb[:, sl * 512:(sl + 1) * 512]),
                            waits=[("AC", ev)], inc="GP", k=16)
                        if dc == DCH - 1:
                            prev_pj_done = pj
                            prev_aost = last_aost[sl]

                    def mk_a5epi(s=s, g=g, prev_aost=prev_aost):
                        def a5epi():
                            for tt in range(4):
                                t = 4 * g + tt
                                row0 = s * N + t * 128
                                par = t % 2
                                tr = E.op("sync",
                                          lambda e, row0=row0, par=par:
                                          e.dma_start_transpose(
                                              out=aot_sb[:, par * D:(par + 1) * D],
                                              in_=sc_ao[:, row0:row0 + 128]),
                                          waits=[("GP", prev_aost),
                                                 ("AC", pget(lm_lnA, s, t))],
                                          inc="HW", k=16)
                                stat, sg, bi = ln_stats(
                                    lambda par=par:
                                    aot_sb[:, par * D:(par + 1) * D],
                                    par, C1_LN, C2_LN, [("HW", tr)], base=64)
                                lnm = E.op("scalar",
                                           lambda e, par=par, sg=sg, bi=bi:
                                           e.activation(
                                               out=ln_sb[:, par * D:(par + 1) * D],
                                               in_=aot_sb[:, par * D:(par + 1) * D],
                                               func=AF.Identity, bias=bi,
                                               scale=sg),
                                           waits=[("DV", stat),
                                                  ("GP", pget(lm_xmst, s, t))],
                                           inc="AC")
                                lm_lnA[(s, t)] = lnm
                                acc = E.op("gpsimd",
                                           lambda e, row0=row0, par=par:
                                           e.dma_start(
                                               out=ln_sb[:, par * D:(par + 1) * D],
                                               in_=x_h[row0:row0 + 128, :],
                                               accum_op=ALU.add),
                                           waits=[("AC", lnm)], inc="GP", k=16)
                                lm_xmst[(s, t)] = E.op(
                                    "gpsimd",
                                    lambda e, row0=row0, par=par: e.dma_start(
                                        out=sc_xm[row0:row0 + 128, :],
                                        in_=ln_sb[:, par * D:(par + 1) * D]),
                                    waits=[("GP", acc)], inc="GP", k=16)
                        return a5epi
                    pend_epiA = mk_a5epi()

            if pend_epiA is not None:
                pend_epiA()
                pend_epiA = None
            emit_prep(len(prep_rounds))
            lm["ffnprep"] = E.cnt["GP"]

            E.op("sync", None, waits=[("HW", E.cnt["HW"])], inc="PH")
            E.op("gpsimd", None, waits=[("GP", E.cnt["GP"])], inc="PH")
            E.op("vector", None, inc="PH")
            E.op("scalar", None, inc="PH")
            E.op("tensor", None, inc="PH")
            lm["bar"] = E.cnt["PH"]

            with nc.Block() as blkA:
                E.replay(blkA)

        with ExitStack() as eb:
            wgT_sb = sb(eb, "wgT", [128, DCH * HID], BF16, is_transpose_dest=True)
            wuT_sb = sb(eb, "wuT", [128, DCH * HID], BF16, is_transpose_dest=True)
            wdT_sb = sb(eb, "wdT", [128, HCH * D], BF16, is_transpose_dest=True)
            ffm_sb = sb(eb, "ffm", [128, DCH * 512], BF16, is_transpose_dest=True)
            h_sb = sb(eb, "h", [128, HCH * 512], BF16)
            silu_sb = dump_sb
            psB = ps(eb, "psB", [128, 2048])
            psD = [ps(eb, "psD0", [128, 512]), ps(eb, "psD1", [128, 512])]

            bw_ = [("PH", lm["bar"])]
            for dc in range(DCH):
                h = E.op("sync",
                         lambda e, dc=dc: e.dma_start_transpose(
                             out=wgT_sb[:, dc * HID:(dc + 1) * HID],
                             in_=sc_wg[:, dc * 128:(dc + 1) * 128]),
                         waits=bw_ + [("GP", lm["ffnprep"])], inc="HW", k=16)
            for dc in range(DCH):
                h = E.op("sync",
                         lambda e, dc=dc: e.dma_start_transpose(
                             out=wuT_sb[:, dc * HID:(dc + 1) * HID],
                             in_=sc_wu[:, dc * 128:(dc + 1) * 128]),
                         inc="HW", k=16)
            for c in range(HCH):
                h = E.op("sync",
                         lambda e, c=c: e.dma_start_transpose(
                             out=wdT_sb[:, c * D:(c + 1) * D],
                             in_=sc_wd[:, c * 128:(c + 1) * 128]),
                         inc="HW", k=16)
            lm["wtB"] = h

            prev_b1_done = 0
            prev_dmm_last = 0
            prev_fost = 0
            last_fost = {0: 0, 1: 0}
            last_sil = {0: 0, 1: 0}
            last_hm = {0: 0, 1: 0}
            last_hm_any = 0
            last_evd = {0: 0, 1: 0}
            lm_fT = {}
            lm_lnB = {}
            lm_lnBe = {}
            lm_yst = {}
            first_b = True

            lm_ffm_map = {}

            def emit_fln(s, g):
                nonlocal first_b
                if True:
                    grow = s * N + g * 512
                    for tt in range(4):
                        t = 4 * g + tt
                        row0 = s * N + t * 128
                        par = t % 2
                        xsl = (slice(None), slice(par * D, (par + 1) * D))
                        w = [("AC", pget(lm_lnB, s, t))]
                        if first_b:
                            w = bw_ + w
                        ld = E.op("sync",
                                  lambda e, row0=row0, xsl=xsl: e.dma_start(
                                      out=xt_sb[xsl], in_=sc_xm[row0:row0 + 128, :]),
                                  waits=w, inc="HW", k=16)
                        wf = [("HW", ld)]
                        if first_b:
                            wf = bw_ + wf
                            first_b = False
                        stat, sg, bi = ln_stats(
                            lambda xsl=xsl: xt_sb[xsl], par, C1_LN, C2_LN, wf, base=32)
                        nrm = E.op("scalar",
                                   lambda e, xsl=xsl, par=par, sg=sg, bi=bi:
                                   e.activation(out=xp_sb[:, par * D:(par + 1) * D],
                                                in_=xt_sb[xsl], func=AF.Identity,
                                                bias=bi, scale=sg),
                                   waits=[("DV", stat),
                                          ("GP", pget(lm_fT, s, t))],
                                   inc="AC")
                        lm_lnB[(s, t)] = nrm
                        lm_fT[(s, t)] = E.op(
                            "gpsimd",
                            lambda e, row0=row0, par=par: e.dma_start(
                                out=sc_xp[row0:row0 + 128, :],
                                in_=xp_sb[:, par * D:(par + 1) * D]),
                            waits=[("AC", nrm)], inc="GP", k=16)

            def emit_ftr(s, g):
                if True:
                    grow = s * N + g * 512
                    for dc in range(DCH):
                        h = E.op("sync",
                                 lambda e, dc=dc, grow=grow: e.dma_start_transpose(
                                     out=ffm_sb[:, dc * 512:(dc + 1) * 512],
                                     in_=sc_xp[grow:grow + 512,
                                               dc * 128:(dc + 1) * 128]),
                                 waits=[("GP", lm_fT[(s, 4 * g + 3)]),
                                        ("PE", prev_b1_done)],
                                 inc="HW", k=16)
                    lm_ffm_map[(s, g)] = h


            def emit_b12(s, g, mid_hook=None):
                nonlocal prev_b1_done, prev_dmm_last, prev_fost, last_hm_any
                if True:
                    grow = s * N + g * 512
                    for c in range(HCH):
                        q0 = (c % 2) * 1024
                        for dc in range(DCH):
                            w = []
                            if dc == 0:
                                w = [("HW", lm_ffm_map[(s, g)]),
                                     ("HW", lm["wtB"]),
                                     ("AC", last_sil[c % 2])]
                            gmm = E.op(
                                "tensor",
                                lambda e, c=c, dc=dc, q0=q0: e.matmul(
                                    psB[:, q0:q0 + 512],
                                    wgT_sb[:, dc * HID + c * 128:
                                           dc * HID + (c + 1) * 128],
                                    ffm_sb[:, dc * 512:(dc + 1) * 512],
                                    start=(dc == 0), stop=(dc == DCH - 1)),
                                waits=w, inc=("PE" if dc == DCH - 1 else None))
                        for dc in range(DCH):
                            w = []
                            if dc == 0:
                                w = [("DV", last_hm[c % 2])]
                                if c == 0:
                                    w.append(("PE", prev_dmm_last))
                            umm = E.op(
                                "tensor",
                                lambda e, c=c, dc=dc, q0=q0: e.matmul(
                                    psB[:, q0 + 512:q0 + 1024],
                                    wuT_sb[:, dc * HID + c * 128:
                                           dc * HID + (c + 1) * 128],
                                    ffm_sb[:, dc * 512:(dc + 1) * 512],
                                    start=(dc == 0), stop=(dc == DCH - 1)),
                                waits=w, inc=("PE" if dc == DCH - 1 else None))
                        last_umm = umm
                        last_sil[c % 2] = E.op(
                            "scalar",
                            lambda e, q0=q0: e.activation(
                                out=silu_sb[:, 0:512], in_=psB[:, q0:q0 + 512],
                                func=AF.Silu, scale=DEQ),
                            waits=[("PE", gmm), ("DV", last_hm_any)],
                            inc="AC")
                        last_hm[c % 2] = last_hm_any = E.op(
                            "vector",
                            lambda e, c=c, q0=q0: e.tensor_tensor(
                                out=h_sb[:, c * 512:(c + 1) * 512],
                                in0=silu_sb[:, 0:512],
                                in1=psB[:, q0 + 512:q0 + 1024],
                                op=ALU.mult),
                            waits=[("AC", last_sil[c % 2]), ("PE", umm)],
                            inc="DV")
                        if c == 5 and mid_hook is not None:
                            mid_hook()
                    lm_hm_all = last_hm_any

                    for dc in range(DCH):
                        for c in range(HCH):
                            w = []
                            if c == 0:
                                w = [("DV", lm_hm_all),
                                     ("AC", last_evd[dc % 2])]
                            dmm = E.op(
                                "tensor",
                                lambda e, dc=dc, c=c: e.matmul(
                                    psD[dc % 2][:, 0:512],
                                    wdT_sb[:, c * D + dc * 128:
                                           c * D + (dc + 1) * 128],
                                    h_sb[:, c * 512:(c + 1) * 512],
                                    start=(c == 0), stop=(c == HCH - 1)),
                                waits=w, inc=("PE" if c == HCH - 1 else None))
                        sl = dc % 2
                        last_evd[sl] = E.op(
                            "scalar",
                            lambda e, sl=sl, dc=dc: e.activation(
                                out=aoT_sb[:, sl * 512:(sl + 1) * 512],
                                in_=psD[sl][:, 0:512], func=AF.Identity,
                                scale=DEQ2),
                            waits=[("PE", dmm), ("GP", last_fost[sl])],
                            inc="AC")
                        last_fost[sl] = E.op(
                            "gpsimd",
                            lambda e, dc=dc, sl=sl, grow=grow: e.dma_start(
                                out=sc_ao[dc * 128:(dc + 1) * 128,
                                          grow:grow + 512],
                                in_=aoT_sb[:, sl * 512:(sl + 1) * 512]),
                            waits=[("AC", last_evd[sl])], inc="GP", k=16)
                        if dc == DCH - 1:
                            prev_dmm_last = dmm
                            prev_fost = last_fost[sl]
                    prev_b1_done = last_umm


            def emit_epi(s, g):
                nonlocal prev_aost
                if True:
                    grow = s * N + g * 512
                    for tt in range(4):
                        t = 4 * g + tt
                        row0 = s * N + t * 128
                        par = t % 2
                        tr = E.op("sync",
                                  lambda e, row0=row0, par=par:
                                  e.dma_start_transpose(
                                      out=aot_sb[:, par * D:(par + 1) * D],
                                      in_=sc_ao[:, row0:row0 + 128]),
                                  waits=[("GP", prev_fost),
                                         ("AC", pget(lm_lnBe, s, t))],
                                  inc="HW", k=16)
                        stat, sg, bi = ln_stats(
                            lambda par=par: aot_sb[:, par * D:(par + 1) * D],
                            par, C1_LN, C2_LN, [("HW", tr)], base=64)
                        lnm = E.op("scalar",
                                   lambda e, par=par, sg=sg, bi=bi: e.activation(
                                       out=ln_sb[:, par * D:(par + 1) * D],
                                       in_=aot_sb[:, par * D:(par + 1) * D],
                                       func=AF.Identity, bias=bi, scale=sg),
                                   waits=[("DV", stat),
                                          ("GP", pget(lm_yst, s, t))],
                                   inc="AC")
                        lm_lnBe[(s, t)] = lnm
                        acc = E.op("gpsimd",
                                   lambda e, row0=row0, par=par: e.dma_start(
                                       out=ln_sb[:, par * D:(par + 1) * D],
                                       in_=sc_xm[row0:row0 + 128, :],
                                       accum_op=ALU.add),
                                   waits=[("AC", lnm)], inc="GP", k=16)
                        lm_yst[(s, t)] = E.op(
                            "gpsimd",
                            lambda e, row0=row0, par=par: e.dma_start(
                                out=y_h[row0:row0 + 128, :],
                                in_=ln_sb[:, par * D:(par + 1) * D]),
                            waits=[("GP", acc)], inc="GP", k=16)


            groups = [(s, g) for s in range(SEQS) for g in range(NG)]
            for pg in groups[:2]:
                emit_fln(*pg)
            emit_ftr(*groups[0])
            for gi, (s, g) in enumerate(groups):
                def mid_hook(gi=gi):
                    if gi + 2 < len(groups):
                        emit_fln(*groups[gi + 2])
                    if gi >= 1:
                        emit_epi(*groups[gi - 1])
                emit_b12(s, g, mid_hook)
                if gi + 1 < len(groups):
                    emit_ftr(*groups[gi + 1])
            emit_epi(*groups[-1])

            E.op("gpsimd", None, waits=[("GP", E.cnt["GP"])])

            with nc.Block() as blkB:
                E.replay(blkB)

    return nc


_NC_CACHE = {}


def kernel(L0_attn_tokens, L0_attn_norm_w, L0_attn_Wq, L0_attn_Wkv, L0_attn_Wout,
           L0_attn_Wg, L0_ffn_w_gate, L0_ffn_w_up, L0_ffn_w_down):
    if "nc" not in _NC_CACHE:
        _NC_CACHE["nc"] = build_program()
    nc = _NC_CACHE["nc"]

    toks = np.ascontiguousarray(L0_attn_tokens, dtype=np.float32)
    nw = np.ascontiguousarray(L0_attn_norm_w, dtype=np.float32).reshape(D, 1)
    wq = np.ascontiguousarray(L0_attn_Wq, dtype=np.float32)
    wkv = np.ascontiguousarray(L0_attn_Wkv, dtype=np.float32)
    wo = np.ascontiguousarray(L0_attn_Wout, dtype=np.float32)
    wg = np.ascontiguousarray(L0_attn_Wg, dtype=np.float32)
    wgi = np.ascontiguousarray(L0_ffn_w_gate, dtype=np.int32)
    wui = np.ascontiguousarray(L0_ffn_w_up, dtype=np.int32)
    wdi = np.ascontiguousarray(L0_ffn_w_down, dtype=np.int32)

    in_maps = []
    for c in range(8):
        shard = np.ascontiguousarray(
            toks[c * SEQS:(c + 1) * SEQS].reshape(ROWS, D))
        in_maps.append({
            "x": shard, "norm_w": nw, "Wq": wq, "Wkv": wkv, "Wout": wo,
            "Wg": wg, "w_gate": wgi, "w_up": wui, "w_down": wdi,
        })
    res = run_bass_kernel_spmd(nc, in_maps, core_ids=list(range(8)))
    out = np.concatenate(
        [r["y"].reshape(SEQS, N, D) for r in res.results], axis=0)
    return out
